# version 26 (final)
# baseline (speedup 1.0000x reference)
import numpy as np
import ml_dtypes
import concourse.bass as bass
import concourse.mybir as mybir
from concourse.bass_utils import run_bass_kernel_spmd
from contextlib import ExitStack

F32 = mybir.dt.float32
BF16 = mybir.dt.bfloat16
I32 = mybir.dt.int32
AF = mybir.ActivationFunctionType
ALU = mybir.AluOpType
AX = mybir.AxisListType
BF = ml_dtypes.bfloat16

D = 1024
T = 2048
NCORES = 8
DCH = 8
NTAU = 4
NT = 512
NEXP = 8
FF = 4096
NFC = 32
NH = 16
HDIM = 64
NKV = 4
PI = float(np.pi)

ALL_STAGES = ("ln0", "attn0", "moe0", "ln1", "attn1", "ffn1")


class PB:
    ENGINES = ("tensor", "vector", "scalar", "gpsimd", "sync")

    def __init__(self, nc, ctx):
        self.nc, self.ctx = nc, ctx
        self.q = {e: [] for e in self.ENGINES}
        self.sems = {}
        self.cnt = {}
        self.waited = {e: {} for e in self.ENGINES}

    def sem(self, name):
        h = self.ctx.enter_context(self.nc.semaphore(name))
        self.sems[name] = h
        self.cnt[name] = 0
        return h

    def tick(self, name, n=1):
        self.cnt[name] += n
        return self.cnt[name]

    def fence(self, exempt=()):
        snap = [(s, self.cnt[s]) for s in self.cnt]
        self.last_fence = snap
        self.fence_exempt = set(exempt)
        if hasattr(self, "ready"):
            for k, st in self.ready.items():
                if k not in exempt:
                    st["fence"] = snap

    def on(self, eng, fn, wait=(), inc=()):
        if getattr(self, "pending", None) and self.pending.get(eng):
            wait = self.pending.pop(eng) + list(wait)
        w = []
        for s, c in wait:
            if c is None or c <= 0:
                continue
            if self.waited[eng].get(s, 0) < c:
                self.waited[eng][s] = c
                w.append((s, c))
        self.q[eng].append((w, fn, list(inc)))

    def _dep_waits(self, reads, writes):
        if not hasattr(self, "ready"):
            self.ready = {}
        w = []
        for k in reads:
            st = self.ready.get(k)
            if st and st["w"]:
                w.append(st["w"])
        for k in writes:
            st = self.ready.get(k)
            if st:
                if st["w"]:
                    w.append(st["w"])
                w.extend(st["r"])
                if st.get("fence"):
                    w.extend(st.pop("fence"))
            elif getattr(self, "last_fence", None) and k not in getattr(self, "fence_exempt", ()):
                w.extend(self.last_fence)
        return w

    def _dep_update(self, reads, writes, tok):
        if not hasattr(self, "ready"):
            self.ready = {}
        for k in reads:
            st = self.ready.setdefault(k, {"w": None, "r": []})
            st["r"].append(tok)
        for k in writes:
            self.ready[k] = {"w": tok, "r": []}

    def dma(self, qeng, out, in_, sem, reads=(), writes=(), wait=(), serialize=None):
        if serialize is None:
            serialize = sem == "dmisc"
        w = list(wait) + self._dep_waits(reads, writes)
        if serialize and self.cnt[sem] > 0:
            w = [(sem, self.cnt[sem])] + w
        c = self.tick(sem, 16)
        self.on(qeng, lambda e, o=out, i=in_: e.dma_start(o, i), wait=w, inc=[(sem, 16)])
        self._dep_update(reads, writes, (sem, c))
        return (sem, c)

    def op(self, eng, fn, reads=(), writes=(), wait=()):
        sem = {"tensor": "pe", "vector": "dv", "scalar": "sc", "gpsimd": "gp"}[eng]
        w = list(wait) + self._dep_waits(reads, writes)
        c = self.tick(sem, 1)
        self.on(eng, fn, wait=w, inc=[(sem, 1)])
        self._dep_update(reads, writes, (sem, c))
        return (sem, c)

    def opq(self, eng, fn, reads=(), writes=(), wait=()):
        w = list(wait) + self._dep_waits(reads, writes)
        self.on(eng, fn, wait=w)

    def emit(self):
        blk = self.ctx.enter_context(self.nc.Block())
        sems = self.sems

        def mk(ops):
            def body(eng):
                for w, fn, incs in ops:
                    for s, c in w:
                        eng.wait_ge(sems[s], c)
                    ins = fn(eng)
                    for s, n in incs:
                        ins.then_inc(sems[s], n)
            return body

        blk.tensor(mk(self.q["tensor"]))
        blk.vector(mk(self.q["vector"]))
        blk.scalar(mk(self.q["scalar"]))
        blk.gpsimd(mk(self.q["gpsimd"]))
        blk.sync(mk(self.q["sync"]))


def ceil_div(a, b):
    return (a + b - 1) // b


def build_nc(stages=ALL_STAGES, dbg=None):
    nc = bass.Bass("TRN2", target_bir_lowering=False)

    xT_d = nc.dram_tensor("xT", [D, T], F32, kind="ExternalInput").ap()
    wq0_d = nc.dram_tensor("l0wq", [D, D], F32, kind="ExternalInput").ap()
    wk0_d = nc.dram_tensor("l0wk", [D, D], F32, kind="ExternalInput").ap()
    bq0_d = nc.dram_tensor("l0bq", [1, D], F32, kind="ExternalInput").ap()
    bk0_d = nc.dram_tensor("l0bk", [1, D], F32, kind="ExternalInput").ap()
    wr0_d = nc.dram_tensor("l0wr", [D, NEXP], F32, kind="ExternalInput").ap()
    w1_d = nc.dram_tensor("l0w1", [NEXP, D, FF], F32, kind="ExternalInput").ap()
    w2_d = nc.dram_tensor("l0w2", [NEXP, FF, D], F32, kind="ExternalInput").ap()
    wq1_d = nc.dram_tensor("l1wq", [D, D], F32, kind="ExternalInput").ap()
    wk1_d = nc.dram_tensor("l1wk", [D, 256], F32, kind="ExternalInput").ap()
    wv1_d = nc.dram_tensor("l1wv", [D, 256], F32, kind="ExternalInput").ap()
    wo1_d = nc.dram_tensor("l1wo", [D, D], F32, kind="ExternalInput").ap()
    wtT_d = nc.dram_tensor("l1wt", [D, D], F32, kind="ExternalInput").ap()
    al1_d = nc.dram_tensor("l1alpha", [1, 1], F32, kind="ExternalInput").ap()
    bi1_d = nc.dram_tensor("l1bias", [1, D], F32, kind="ExternalInput").ap()
    sel_d = nc.dram_tensor("sel8", [NEXP, NEXP * 128], F32, kind="ExternalInput").ap()
    hmask_d = nc.dram_tensor("hmask", [128, 2], BF16, kind="ExternalInput").ap()
    sel2_d = nc.dram_tensor("sel2c", [2, 256], F32, kind="ExternalInput").ap()
    idf32_d = nc.dram_tensor("idf32c", [128, 128], F32, kind="ExternalInput").ap()
    cmask_d = nc.dram_tensor("cmask", [128, 4 * NT], BF16, kind="ExternalInput").ap()
    out_d = nc.dram_tensor("outT", [D, T], F32, kind="ExternalOutput").ap()
    sw1_d = nc.dram_tensor("sw1", [NEXP, 8, 128, 4096], BF16).ap()
    sw2_d = nc.dram_tensor("sw2", [NEXP, 8, 128, 4096], BF16).ap()

    xT_t = xT_d.rearrange("(c p) t -> c p t", p=128)
    out_t = out_d.rearrange("(c p) t -> c p t", p=128)

    with ExitStack() as ctx:
        pb = PB(nc, ctx)
        for s in ("din", "dst", "pe", "dv", "sc", "gp",
                  "w1a", "w1b", "w2a", "w2b", "stga", "stgb", "ppw", "dmisc",
                  "pga", "pgb"):
            pb.sem(s)

        sb = lambda name, shape, dt: ctx.enter_context(nc.sbuf_tensor(name, shape, dt))
        ps = lambda name, shape, dt: ctx.enter_context(nc.psum_tensor(name, shape, dt))

        xT = sb("xTs", [128, DCH * T], F32)
        xnT = sb("xnTs", [128, DCH * T], BF16)
        t1s = sb("t1s", [128, NT], F32)
        onesb = sb("onesb", [1, NT], BF16)
        onesf = sb("onesf", [1, 128], F32)
        invN = sb("invN", [128, 1], BF16)
        onescol = sb("onescol", [128, 1], BF16)
        epscol = sb("epscol", [1, 1], F32)
        colpool = sb("colpool", [128, 32], F32)
        hmask = sb("hmaskS", [128, 2], BF16)
        bankA = ps("bankA", [128, 4 * NT], F32)
        bankB = ps("bankB", [128, 2 * NT], F32)
        bank = [bankA[:, i * NT:(i + 1) * NT] for i in range(4)] + \
               [bankB[:, 0:NT], bankB[:, NT:2 * NT]] + \
               [ps(f"bank{i}", [128, NT], F32) for i in range(6, 8)]

        xch = lambda dc: xT[:, dc * T:(dc + 1) * T]
        xnch = lambda dc: xnT[:, dc * T:(dc + 1) * T]

        for dc in range(DCH):
            pb.dma("sync", xch(dc), xT_t[dc], "din")
        pb.dma("sync", hmask[:], hmask_d[:, :], "din")
        din_inputs = ("din", pb.cnt["din"])
        pb._dep_update((), [f"x{dc}" for dc in range(DCH)] + ["hmask"], din_inputs)

        pb.op("vector", lambda e: e.memset(onesb[:], 1.0), writes=["onesb"])
        pb.op("vector", lambda e: e.memset(onesf[:], 1.0), writes=["onesf"])
        pb.op("vector", lambda e: e.memset(invN[:], float(1.0 / D)), writes=["invN"])
        pb.op("vector", lambda e: e.memset(epscol[:], float(1e-5)), writes=["eps"])
        consts_done = pb.op("vector", lambda e: e.memset(onescol[:], 1.0),
                            writes=["onescol"])

        def emit_layernorm(tag, xT_ready):
            lctx = ExitStack()
            lsb = lambda name, shape, dt: lctx.enter_context(nc.sbuf_tensor(name, shape, dt))
            rows = lsb(f"rows_{tag}", [1, 5 * NT], F32)
            xbf = lsb(f"xbf_{tag}", [128, NT], BF16)
            sqb = lsb(f"sqb_{tag}", [128, NT], BF16)
            rb = lsb(f"rb_{tag}", [128, NT], F32)
            mrb = lsb(f"mrb_{tag}", [128, NT], F32)
            r_mean = rows[0:1, 0 * NT:1 * NT]
            r_ex2 = rows[0:1, 1 * NT:2 * NT]
            r_var = rows[0:1, 2 * NT:3 * NT]
            r_r = rows[0:1, 3 * NT:4 * NT]
            r_mr = rows[0:1, 4 * NT:5 * NT]

            g = tag
            for tau in range(NTAU):
                t0 = tau * NT
                for dc in range(DCH):
                    pb.op("vector", lambda e, dc=dc, t0=t0: e.tensor_copy(
                        xbf[:], xch(dc)[:, t0:t0 + NT]),
                        reads=[f"x{dc}"], writes=[g + "xbf"], wait=list(xT_ready))
                    pb.op("vector", lambda e: e.tensor_tensor(
                        sqb[:], xbf[:], xbf[:], ALU.mult),
                        reads=[g + "xbf"], writes=[g + "sqb"])
                    pb.op("tensor", lambda e, dc=dc: e.matmul(
                        bank[5][0:1, :], invN[:, :], xbf[:],
                        start=(dc == 0), stop=(dc == DCH - 1)),
                        reads=[g + "xbf", "invN"],
                        writes=["b5"] if dc == DCH - 1 else [],
                        wait=(pb._dep_waits((), ["b5"]) if dc == 0 else []))
                    pb.op("tensor", lambda e, dc=dc: e.matmul(
                        bank[6][0:1, :], invN[:, :], sqb[:],
                        start=(dc == 0), stop=(dc == DCH - 1)),
                        reads=[g + "sqb", "invN"],
                        writes=["b6"] if dc == DCH - 1 else [],
                        wait=(pb._dep_waits((), ["b6"]) if dc == 0 else []))
                pb.op("scalar", lambda e: e.copy(r_mean, bank[5][0:1, :]),
                      reads=["b5"], writes=[g + "rm"])
                pb.op("scalar", lambda e: e.copy(r_ex2, bank[6][0:1, :]),
                      reads=["b6"], writes=[g + "re"])
                pb.op("vector", lambda e: e.tensor_tensor(
                    r_var, r_mean, r_mean, ALU.mult),
                    reads=[g + "rm"], writes=[g + "rv"])
                pb.op("vector", lambda e: e.tensor_tensor(
                    r_var, r_ex2, r_var, ALU.subtract),
                    reads=[g + "re", g + "rv"], writes=[g + "rv"])
                pb.op("scalar", lambda e: e.activation(
                    r_var, r_var, AF.Ln, bias=epscol[:]),
                    reads=[g + "rv", "eps"], writes=[g + "rv"])
                pb.op("scalar", lambda e: e.activation(
                    r_r, r_var, AF.Exp, scale=-0.5),
                    reads=[g + "rv"], writes=[g + "rr"])
                pb.op("vector", lambda e: e.tensor_tensor(
                    r_mr, r_mean, r_r, ALU.mult),
                    reads=[g + "rm", g + "rr"], writes=[g + "rmr"])
                pb.op("tensor", lambda e: e.matmul(
                    bank[7][:, :], onesf[:, :], r_r, start=True, stop=True),
                    reads=[g + "rr", "onesf"], writes=["b7"])
                pb.op("vector", lambda e: e.tensor_copy(rb[:], bank[7][:, :]),
                      reads=["b7"], writes=[g + "rb"])
                pb.op("tensor", lambda e: e.matmul(
                    bank[7][:, :], onesf[:, :], r_mr, start=True, stop=True),
                    reads=[g + "rmr", "onesf"], writes=["b7"])
                pb.op("vector", lambda e: e.tensor_copy(mrb[:], bank[7][:, :]),
                      reads=["b7"], writes=[g + "mrb"])
                for dc in range(DCH):
                    pb.op("vector", lambda e, dc=dc, t0=t0: e.tensor_tensor(
                        t1s[:], xch(dc)[:, t0:t0 + NT], rb[:], ALU.mult),
                        reads=[f"x{dc}", g + "rb"], writes=["t1s"])
                    pb.op("vector", lambda e, dc=dc, t0=t0: e.tensor_tensor(
                        xnch(dc)[:, t0:t0 + NT], t1s[:], mrb[:], ALU.subtract),
                        reads=["t1s", g + "mrb"], writes=[f"xn{dc}.{tau}"])
            lctx.close()
            pb.fence()
            return ("dv", pb.cnt["dv"])

        stg = sb("stg", [128, 2 * 2048], F32)
        castb = sb("castb", [128, 2048], BF16)
        wrstg = sb("wrstg", [128, 64], F32)

        def load_cast(dst_bf, src_rows, nchunks, width, tagp, scale=None, dma_q="sync"):
            for i in range(nchunks):
                slot = i % 2
                sem = "stga" if slot == 0 else "stgb"
                pb.dma(dma_q, stg[:, slot * 2048: slot * 2048 + width], src_rows(i),
                       sem, writes=[f"stg{slot}"])
                if scale is None:
                    pb.op("vector", lambda e, i=i, slot=slot: e.tensor_copy(
                        dst_bf[:, i * width:(i + 1) * width],
                        stg[:, slot * 2048: slot * 2048 + width]),
                        reads=[f"stg{slot}"], writes=[f"{tagp}{i}"])
                else:
                    pb.op("vector", lambda e, i=i, slot=slot: e.tensor_scalar(
                        dst_bf[:, i * width:(i + 1) * width],
                        stg[:, slot * 2048: slot * 2048 + width],
                        float(scale), None, ALU.mult),
                        reads=[f"stg{slot}"], writes=[f"{tagp}{i}"])

        def attn_stream_gen(tiles):
            pending_fin = [None, None]

            def flush():
                for h12 in range(2):
                    if pending_fin[h12]:
                        pending_fin[h12]()
                        pending_fin[h12] = None

            for ti, t in enumerate(tiles):
                nkc = t["nkc"]
                for kc in range(nkc + 1):
                    if kc < nkc:
                        pp = 2 * (kc % 2)
                        for h12 in range(2):
                            q4 = pp + h12
                            pb.op("tensor",
                                  lambda e, kc=kc, h12=h12, q4=q4, t=t: e.matmul(
                                      bank[q4][:, :], t["klhsT"](kc, h12),
                                      t["qrhs"][h12], start=True, stop=True),
                                  reads=t["kreads"] + [t["qkey"]], writes=[f"b{q4}"])
                    if kc == 1:
                        flush()
                    if kc >= 1:
                        kp = kc - 1
                        qp = 2 * (kp % 2)
                        for h12 in range(2):
                            q4 = qp + h12
                            pb.op("tensor",
                                  lambda e, kp=kp, h12=h12, q4=q4, t=t, nkc=nkc: e.matmul(
                                      bank[4 + h12][0:65, :], t["vap"](kp, h12),
                                      t["expb"][:, q4 * NT:(q4 + 1) * NT],
                                      start=(kp == 0), stop=(kp == nkc - 1)),
                                  reads=[f"e{q4}", t["vkeys"][h12]],
                                  writes=([f"b{4 + h12}"] if kp == nkc - 1 else []),
                                  wait=(pb._dep_waits((), [f"b{4 + h12}"]) if kp == 0 else []))
                    if kc < nkc:
                        pp = 2 * (kc % 2)
                        pb.op("scalar",
                              lambda e, pp=pp, t=t: e.activation(
                                  t["expb"][:, pp * NT:(pp + 2) * NT],
                                  bankA[:, pp * NT:(pp + 2) * NT], AF.Exp,
                                  bias=t["negcol"]),
                              reads=[f"b{pp}", f"b{pp + 1}", t["ckey"]],
                              writes=[f"e{pp}", f"e{pp + 1}"])
                        if t.get("mask_from") is not None and kc >= t["mask_from"]:
                            o = kc - t["mask_from"]
                            for h12 in range(2):
                                q4 = pp + h12
                                pb.op("vector",
                                      lambda e, q4=q4, o=o, t=t: e.tensor_tensor(
                                          t["expb"][:, q4 * NT:(q4 + 1) * NT],
                                          t["expb"][:, q4 * NT:(q4 + 1) * NT],
                                          t["cmask"][:, o * NT:(o + 1) * NT], ALU.mult),
                                      reads=[f"e{q4}", "cmask"], writes=[f"e{q4}"])
                    yield
                for h12 in range(2):
                    pending_fin[h12] = (lambda t=t, h12=h12: t["fins"][h12]())
            flush()

        def attn_stream(tiles):
            for _ in attn_stream_gen(tiles):
                pass

        def emit_attn0(xn_done):
            lctx = ExitStack()
            lsb = lambda name, shape, dt: lctx.enter_context(nc.sbuf_tensor(name, shape, dt))
            wqbf = lsb("wqbf", [128, DCH * 1024], BF16)
            wkbf = lsb("wkbf", [128, DCH * 1024], BF16)
            bqbf = lsb("bqbf", [1, D], BF16)
            bkbf = lsb("bkbf", [1, D], BF16)
            qTb = [lsb("qT0a", [128, T], BF16), lsb("qT0b", [128, T], BF16)]
            kTb = [lsb("kT0a", [128, T], BF16), lsb("kT0b", [128, T], BF16)]
            vAb = [lsb("vA0a", [128, 16 * 65], BF16), lsb("vA0b", [128, 16 * 65], BF16)]
            vBb = [lsb("vB0a", [128, 16 * 65], BF16), lsb("vB0b", [128, 16 * 65], BF16)]
            expb = lsb("exp0", [128, 4 * NT], BF16)
            rrow = lsb("rrow0", [1, 2 * NT], F32)
            rhl = lsb("rhl0", [1, 4 * NT], BF16)
            rb2 = lsb("rb20", [128, 2 * NT], F32)

            wq_r = wq0_d.rearrange("(c p) n -> c p n", p=128)
            wk_r = wk0_d.rearrange("(c p) n -> c p n", p=128)
            load_cast(wqbf, lambda i: wq_r[i], DCH, 1024, "wq", scale=1.0 / 32.0)
            load_cast(wkbf, lambda i: wk_r[i], DCH, 1024, "wk")
            pb.dma("sync", rrow[0:1, 0:NT], bq0_d[0:1, 0:NT], "dmisc")
            pb.dma("sync", rrow[0:1, NT:2 * NT], bq0_d[0:1, NT:2 * NT], "dmisc")
            bq_half = ("dmisc", pb.cnt["dmisc"])
            pb._dep_update((), ["bqstg"], bq_half)
            pb.op("vector", lambda e: e.tensor_scalar(
                bqbf[0:1, 0:NT], rrow[0:1, 0:NT], 1.0 / 32.0, None, ALU.mult),
                reads=["bqstg"], writes=["bqbf"])
            pb.op("vector", lambda e: e.tensor_scalar(
                bqbf[0:1, NT:2 * NT], rrow[0:1, NT:2 * NT], 1.0 / 32.0, None, ALU.mult),
                reads=["bqstg"], writes=["bqbf"])
            pb.dma("sync", rrow[0:1, 0:NT], bk0_d[0:1, 0:NT], "dmisc",
                   reads=["bqbf"])
            pb.dma("sync", rrow[0:1, NT:2 * NT], bk0_d[0:1, NT:2 * NT], "dmisc",
                   reads=["bqbf"])
            bk_half = ("dmisc", pb.cnt["dmisc"])
            pb._dep_update((), ["bkstg"], bk_half)
            pb.op("vector", lambda e: e.tensor_copy(bkbf[0:1, 0:NT], rrow[0:1, 0:NT]),
                  reads=["bkstg"], writes=["bkbf"])
            pb.op("vector", lambda e: e.tensor_copy(bkbf[0:1, NT:2 * NT], rrow[0:1, NT:2 * NT]),
                  reads=["bkstg"], writes=["bkbf", "rrow0", "rrow1"])
            for par in range(2):
                for vbuf, key in ((vAb[par], f"vA{par}"), (vBb[par], f"vB{par}")):
                    for j in range(16):
                        pb.op("vector", lambda e, vbuf=vbuf, j=j: e.memset(
                            vbuf[:, j * 65 + 64: j * 65 + 65], 1.0),
                            writes=[key + "ones"])

            xnr = [xn_done]
            CP = colpool

            def mk_fin(h12, tgt_fn):
                def fin():
                    avb = 4 + h12
                    bb = h12
                    hi = rhl[0:1, (2 * h12) * NT:(2 * h12 + 1) * NT]
                    lo = rhl[0:1, (2 * h12 + 1) * NT:(2 * h12 + 2) * NT]
                    rr = rrow[0:1, h12 * NT:(h12 + 1) * NT]
                    pb.op("scalar", lambda e: e.activation(
                        rr, bank[avb][64:65, :], AF.Ln),
                        reads=[f"b{avb}"], writes=[f"rrow{h12}"])
                    pb.op("scalar", lambda e: e.activation(
                        rr, rr, AF.Exp, scale=-1.0),
                        reads=[f"rrow{h12}"], writes=[f"rrow{h12}"])
                    pb.op("vector", lambda e: e.tensor_copy(hi, rr),
                          reads=[f"rrow{h12}"], writes=[f"rhi{h12}"])
                    pb.op("vector", lambda e: e.tensor_tensor(lo, rr, hi, ALU.subtract),
                          reads=[f"rrow{h12}", f"rhi{h12}"], writes=[f"rlo{h12}"])
                    pb.op("tensor", lambda e: e.matmul(
                        bank[bb][:, :], onesb[0:1, 0:128], hi, start=True, stop=False),
                        reads=[f"rhi{h12}", "onesb"],
                        wait=pb._dep_waits((), [f"b{bb}"]))
                    pb.op("tensor", lambda e: e.matmul(
                        bank[bb][:, :], onesb[0:1, 0:128], lo, start=False, stop=True),
                        reads=[f"rlo{h12}", "onesb"], writes=[f"b{bb}"])
                    rbs = rb2[:, h12 * NT:(h12 + 1) * NT]
                    pb.op("vector", lambda e: e.tensor_copy(rbs, bank[bb][:, :]),
                          reads=[f"b{bb}"], writes=[f"rb{h12}"])
                    tgt_fn(avb, rbs, h12)
                return fin

            def l0_tgt(hp, hbase, t0):
                def tgt(avb, rbs, h12):
                    pb.op("vector", lambda e, avb=avb, rbs=rbs: e.tensor_tensor(
                        t1s[hbase:hbase + 64, :], bank[avb][0:64, :],
                        rbs[0:64, :], ALU.mult),
                        reads=[f"b{avb}", f"rb{h12}"], writes=["t1s"])
                    pb.op("vector", lambda e: e.tensor_tensor(
                        xch(hp)[hbase:hbase + 64, t0:t0 + NT],
                        xch(hp)[hbase:hbase + 64, t0:t0 + NT],
                        t1s[hbase:hbase + 64, :], ALU.add),
                        reads=["t1s", f"x{hp}"], writes=[f"x{hp}"])
                return tgt

            def prologue_chunks(hp):
                par = hp % 2
                qT, kT, vA, vB = qTb[par], kTb[par], vAb[par], vBb[par]
                ch = []
                for (dst, wbf, bbf, wkey, bkey, dkey) in (
                        (qT, wqbf, bqbf, "wq", "bqbf", f"qT{par}"),
                        (kT, wkbf, bkbf, "wk", "bkbf", f"kT{par}")):
                    for tau in range(NTAU):
                        def proj(tau=tau, dst=dst, wbf=wbf, bbf=bbf,
                                 wkey=wkey, bkey=bkey, dkey=dkey, hp=hp):
                            t0 = tau * NT
                            bnk = bank[6 + tau % 2]
                            bkkey = f"b{6 + tau % 2}"
                            for dc in range(DCH):
                                pb.op("tensor", lambda e, dc=dc, t0=t0, wbf=wbf, hp=hp, bnk=bnk:
                                      e.matmul(bnk[:, :],
                                               wbf[:, dc * 1024 + hp * 128: dc * 1024 + (hp + 1) * 128],
                                               xnch(dc)[:, t0:t0 + NT],
                                               start=(dc == 0), stop=False),
                                      reads=[f"{wkey}{dc}", f"xn{dc}.{tau}"],
                                      wait=(pb._dep_waits((), [bkkey]) if dc == 0 else []) + xnr)
                            pb.op("tensor", lambda e, t0=t0, bbf=bbf, hp=hp, bnk=bnk:
                                  e.matmul(bnk[:, :],
                                           bbf[0:1, hp * 128:(hp + 1) * 128], onesb[0:1, :],
                                           start=False, stop=True),
                                  reads=[bkey, "onesb"], writes=[bkkey])
                            pb.op("scalar", lambda e, t0=t0, dst=dst, bnk=bnk: e.copy(
                                dst[:, t0:t0 + NT], bnk[:, :]),
                                reads=[bkkey], writes=[dkey])
                        ch.append(proj)
                for mt in range(16):
                    def ktm(mt=mt, hp=hp, par=par, vA=vA, vB=vB):
                        bnk = bank[6 + (mt % 2)]
                        bkkey = f"b{6 + (mt % 2)}"
                        for dc in range(DCH):
                            pb.op("tensor", lambda e, dc=dc, mt=mt, hp=hp, bnk=bnk:
                                  e.matmul(bnk[:, 0:128],
                                           xnch(dc)[:, mt * 128:(mt + 1) * 128],
                                           wkbf[:, dc * 1024 + hp * 128: dc * 1024 + (hp + 1) * 128],
                                           start=(dc == 0), stop=False),
                                  reads=[f"wk{dc}"] + [f"xn{dc}.{mt // 4}"],
                                  wait=(pb._dep_waits((), [bkkey]) if dc == 0 else []) + xnr)
                        pb.op("tensor", lambda e, mt=mt, hp=hp, bnk=bnk:
                              e.matmul(bnk[:, 0:128], onesb[0:1, 0:128],
                                       bkbf[0:1, hp * 128:(hp + 1) * 128],
                                       start=False, stop=True),
                              reads=["bkbf", "onesb"], writes=[bkkey])
                        pb.op("scalar", lambda e, mt=mt, bnk=bnk, vA=vA: e.copy(
                            vA[:, mt * 65: mt * 65 + 64], bnk[:, 0:64]),
                            reads=[bkkey, f"vA{par}ones"], writes=[f"vA{par}"])
                        pb.op("scalar", lambda e, mt=mt, bnk=bnk, vB=vB: e.copy(
                            vB[:, mt * 65: mt * 65 + 64], bnk[:, 64:128]),
                            reads=[bkkey, f"vB{par}ones"], writes=[f"vB{par}"])
                    ch.append(ktm)

                def norms(hp=hp, par=par, qT=qT, kT=kT):
                    for (src, skey, coli) in ((qT, f"qT{par}", 0), (kT, f"kT{par}", 4)):
                        pb.op("vector", lambda e, src=src: e.tensor_tensor(
                            expb[:, :], src[:], src[:], ALU.mult),
                            reads=[skey], writes=["e0", "e1", "e2", "e3"])
                        for tau in range(NTAU):
                            pb.op("tensor", lambda e, tau=tau: e.matmul(
                                bank[7][0:1, :], onescol[:, :],
                                expb[:, tau * NT:(tau + 1) * NT], start=True, stop=True),
                                reads=[f"e{tau}", "onescol"], writes=["b7"])
                            pb.op("vector", lambda e, tau=tau, coli=coli: e.reduce_max(
                                CP[0:1, coli + tau:coli + tau + 1], bank[7][0:1, :], AX.X),
                                reads=["b7"], writes=[f"cp{coli + tau}"])
                    pb.op("vector", lambda e: e.reduce_max(CP[0:1, 8:9], CP[0:1, 0:4], AX.X),
                          reads=[f"cp{i}" for i in range(4)], writes=["cpq"])
                    pb.op("vector", lambda e: e.reduce_max(CP[0:1, 9:10], CP[0:1, 4:8], AX.X),
                          reads=[f"cp{i}" for i in range(4, 8)], writes=["cpk"])
                    pb.op("vector", lambda e: e.tensor_tensor(
                        CP[0:1, 10:11], CP[0:1, 8:9], CP[0:1, 9:10], ALU.mult),
                        reads=["cpq", "cpk"], writes=["cpm"])
                    pb.op("scalar", lambda e: e.activation(
                        CP[0:1, 10:11], CP[0:1, 10:11], AF.Ln),
                        reads=["cpm"], writes=["cpm"])
                    pb.op("scalar", lambda e: e.activation(
                        CP[0:1, 10:11], CP[0:1, 10:11], AF.Exp, scale=0.5),
                        reads=["cpm"], writes=["cpm"])
                    pb.op("vector", lambda e: e.tensor_scalar(
                        CP[0:1, 11:12], CP[0:1, 10:11], -1.0, None, ALU.mult),
                        reads=["cpm"], writes=["cpn"])
                    pb.op("tensor", lambda e: e.matmul(
                        bank[7][:, 0:1], onesf[:, :], CP[0:1, 11:12],
                        start=True, stop=True),
                        reads=["cpn", "onesf"], writes=["b7"])
                    pb.op("vector", lambda e, par=par: e.tensor_copy(
                        CP[:, 12 + par:13 + par], bank[7][:, 0:1]),
                        reads=["b7"], writes=[f"neg{par}"])
                ch.append(norms)
                return ch

            def build_tiles(hp):
                par = hp % 2
                qT, kT, vA, vB = qTb[par], kTb[par], vAb[par], vBb[par]
                tiles = []
                for tau in range(NTAU):
                    t0 = tau * NT
                    tiles.append(dict(
                        nkc=16,
                        klhsT=(lambda kc, h12, kT=kT: kT[64 * h12:64 * h12 + 64, kc * 128:(kc + 1) * 128]),
                        qrhs=[qT[0:64, t0:t0 + NT], qT[64:128, t0:t0 + NT]],
                        kreads=[f"kT{par}"], qkey=f"qT{par}",
                        vap=(lambda kc, h12, vA=vA, vB=vB: (vA if h12 == 0 else vB)[:, kc * 65:(kc + 1) * 65]),
                        vkeys=[f"vA{par}", f"vB{par}"], expb=expb,
                        negcol=CP[:, 12 + par:13 + par], ckey=f"neg{par}",
                        fins=[mk_fin(0, l0_tgt(hp, 0, t0)),
                              mk_fin(1, l0_tgt(hp, 64, t0))],
                    ))
                return tiles

            gen = None
            for hp in range(DCH):
                for chf in prologue_chunks(hp):
                    chf()
                    if gen is not None:
                        for _ in range(4):
                            next(gen, None)
                if gen is not None:
                    for _ in gen:
                        pass
                gen = attn_stream_gen(build_tiles(hp))
            for _ in gen:
                pass
            lctx.close()
            pb.fence(exempt=("stg0", "stg1", "castb0"))

        def emit_moe0(xn_done):
            lctx = ExitStack()
            lsb = lambda name, shape, dt: lctx.enter_context(nc.sbuf_tensor(name, shape, dt))
            wrbf = lsb("wrbf", [128, DCH * NEXP], BF16)
            gtm = lsb("gtm", [128, 128], F32)
            ctm = lsb("ctm", [128, 128], F32)
            cT = lsb("cT", [NEXP, T], F32)
            selb = lsb("selb", [NEXP, NEXP * 128], F32)
            idf32 = lsb("idf32", [128, 128], F32)
            hid = lsb("hid", [128, NFC * NT], BF16)
            w1buf = lsb("w1buf", [128, 2 * 4096], BF16)
            w2buf = lsb("w2buf", [128, 2 * 4096], BF16)
            cbs = lsb("cbs", [128, NT], F32)
            mtmp = lsb("mtmp", [128, 2 * NT], F32)
            CP = colpool
            xnr = [xn_done]

            pb.dma("sync", selb[:, :], sel_d[:, :], "dmisc", writes=["selb"])
            pb.dma("sync", idf32[:, :], idf32_d[:, :], "dmisc", writes=["idf32"])
            pb.dma("sync", wrstg[:, :].rearrange("p (c n) -> p c n", n=NEXP),
                   wr0_d.rearrange("(c p) n -> p c n", p=128), "dmisc",
                   writes=["wrstg"])
            pb.op("vector", lambda e: e.tensor_copy(wrbf[:, :], wrstg[:, :]),
                  reads=["wrstg"], writes=["wrbf"])

            for mt in range(16):
                bnk = bank[2 + mt % 2]
                bkkey = f"b{2 + mt % 2}"
                for dc in range(DCH):
                    pb.op("tensor", lambda e, dc=dc, mt=mt, bnk=bnk: e.matmul(
                        bnk[:, 0:NEXP],
                        xnch(dc)[:, mt * 128:(mt + 1) * 128],
                        wrbf[:, dc * NEXP:(dc + 1) * NEXP],
                        start=(dc == 0), stop=(dc == DCH - 1)),
                        reads=["wrbf", f"xn{dc}.{mt // 4}"],
                        writes=[bkkey] if dc == DCH - 1 else [],
                        wait=(pb._dep_waits((), [bkkey]) if dc == 0 else []) + xnr)
                pb.op("vector", lambda e, bnk=bnk: e.tensor_copy(gtm[:, 0:NEXP], bnk[:, 0:NEXP]),
                      reads=[bkkey], writes=["gtm"])
                pb.op("vector", lambda e: e.reduce_max(CP[:, 12:13], gtm[:, 0:NEXP], AX.X),
                      reads=["gtm"], writes=["c12"])
                pb.op("vector", lambda e: e.tensor_scalar(
                    ctm[:, 0:NEXP], gtm[:, 0:NEXP], CP[:, 12:13], None, ALU.is_ge),
                    reads=["gtm", "c12"], writes=["mask1"])
                pb.op("vector", lambda e: e.scalar_tensor_tensor(
                    gtm[:, 8:16], ctm[:, 0:NEXP], -1e30, gtm[:, 0:NEXP],
                    ALU.mult, ALU.add),
                    reads=["gtm", "mask1"], writes=["gwo"])
                pb.op("vector", lambda e: e.reduce_max(CP[:, 13:14], gtm[:, 8:16], AX.X),
                      reads=["gwo"], writes=["c13"])
                pb.op("vector", lambda e: e.tensor_scalar(
                    ctm[:, 8:16], gtm[:, 8:16], CP[:, 13:14], None, ALU.is_ge),
                    reads=["gwo", "c13"], writes=["mask2"])
                pb.op("vector", lambda e: e.tensor_tensor(
                    CP[:, 14:15], CP[:, 13:14], CP[:, 12:13], ALU.subtract),
                    reads=["c12", "c13"], writes=["c14"])
                pb.op("scalar", lambda e: e.activation(
                    CP[:, 14:15], CP[:, 14:15], AF.Exp),
                    reads=["c14"], writes=["c14"])
                pb.op("vector", lambda e: e.tensor_scalar(
                    CP[:, 15:16], CP[:, 14:15], 1.0, None, ALU.add),
                    reads=["c14"], writes=["c15"])
                pb.op("vector", lambda e: e.reciprocal(CP[:, 15:16], CP[:, 15:16]),
                      reads=["c15"], writes=["c15"])
                pb.op("vector", lambda e: e.tensor_tensor(
                    CP[:, 14:15], CP[:, 14:15], CP[:, 15:16], ALU.mult),
                    reads=["c14", "c15"], writes=["c14"])
                pb.op("vector", lambda e: e.tensor_scalar(
                    ctm[:, 0:NEXP], ctm[:, 0:NEXP], CP[:, 15:16], None, ALU.mult),
                    reads=["mask1", "c15"], writes=["mask1"])
                pb.op("vector", lambda e: e.scalar_tensor_tensor(
                    ctm[:, 0:NEXP], ctm[:, 8:16], CP[:, 14:15], ctm[:, 0:NEXP],
                    ALU.mult, ALU.add),
                    reads=["mask1", "mask2", "c14"], writes=["ctile"])
                pb.op("tensor", lambda e: e.transpose(
                    bank[4][0:NEXP, 0:128], ctm[:, 0:NEXP], idf32[:, :]),
                    reads=["ctile", "idf32"], writes=["b4"])
                pb.op("vector", lambda e, mt=mt: e.tensor_copy(
                    cT[0:NEXP, mt * 128:(mt + 1) * 128], bank[4][0:NEXP, 0:128]),
                    reads=["b4"], writes=["cT"])

            for ex in range(NEXP):
                for fg in range(8):
                    for half in range(2):
                        slot = (fg * 2 + half) % 2
                        pb.dma("gpsimd",
                               stg[:, slot * 2048:(slot + 1) * 2048].rearrange(
                                   "p (c f) -> p c f", f=NT),
                               w1_d[ex].rearrange("(c p) (g f) -> g p c f",
                                                  p=128, f=NT)[fg][:, half * 4:(half + 1) * 4, :],
                               "pga" if slot == 0 else "pgb",
                               writes=[f"stg{slot}"])
                        pb.op("gpsimd", lambda e, slot=slot: e.tensor_copy(
                            castb[:, 0:2048],
                            stg[:, slot * 2048:(slot + 1) * 2048]),
                            reads=[f"stg{slot}"], writes=["castb0"])
                        pb.dma("gpsimd", sw1_d[ex, fg, :, half * 2048:(half + 1) * 2048],
                               castb[:, 0:2048],
                               "ppw", reads=["castb0"], serialize=True)
                for dc in range(8):
                    for qt in range(4):
                        slot = qt % 2
                        pb.dma("gpsimd",
                               stg[:, slot * 2048 + 0:slot * 2048 + 1024].rearrange(
                                   "p (fc j) -> p fc j", j=128),
                               w2_d[ex].rearrange("(fc p) (c j) -> c p fc j",
                                                  p=128, j=128)[dc][:, qt * 8:(qt + 1) * 8, :],
                               "pga" if slot == 0 else "pgb",
                               writes=[f"stg{slot}"])
                        pb.op("gpsimd", lambda e, slot=slot: e.tensor_copy(
                            castb[:, 0:1024],
                            stg[:, slot * 2048:slot * 2048 + 1024]),
                            reads=[f"stg{slot}"], writes=["castb0"])
                        pb.dma("gpsimd", sw2_d[ex, dc, :, qt * 1024:(qt + 1) * 1024],
                               castb[:, 0:1024],
                               "ppw", reads=["castb0"], serialize=True)
                pb._dep_update((), [f"pp{ex}"], ("ppw", pb.cnt["ppw"]))

            for ex in range(NEXP):
                for tau in range(NTAU):
                    t0 = tau * NT
                    pb.op("tensor", lambda e, ex=ex, t0=t0: e.matmul(
                        bank[5][:, :], selb[0:NEXP, ex * 128:(ex + 1) * 128],
                        cT[0:NEXP, t0:t0 + NT], start=True, stop=True),
                        reads=["cT", "selb"], writes=["b5"])
                    pb.op("vector", lambda e: e.tensor_copy(cbs[:], bank[5][:, :]),
                          reads=["b5"], writes=["cbs"])
                    for fg in range(8):
                        slot = fg % 2
                        wsem = "w1a" if slot == 0 else "w1b"
                        pb.dma("sync", w1buf[:, slot * 4096:(slot + 1) * 4096],
                               sw1_d[ex, fg], wsem,
                               reads=[f"pp{ex}"], writes=[f"w1s{slot}"])
                        for fp in range(4):
                            f = fg * 4 + fp
                            rot = (fg * 4 + fp) % 4
                            bnk = bank[rot]
                            bkkey = f"b{rot}"
                            for dc in range(DCH):
                                pb.op("tensor", lambda e, dc=dc, fp=fp, slot=slot, t0=t0, bnk=bnk:
                                      e.matmul(bnk[:, :],
                                               w1buf[:, slot * 4096 + dc * NT + fp * 128:
                                                      slot * 4096 + dc * NT + (fp + 1) * 128],
                                               xnch(dc)[:, t0:t0 + NT],
                                               start=(dc == 0), stop=(dc == DCH - 1)),
                                      reads=[f"w1s{slot}", f"xn{dc}.{tau}"],
                                      writes=[bkkey] if dc == DCH - 1 else [],
                                      wait=(pb._dep_waits((), [bkkey]) if dc == 0 else []) + xnr)
                            pb.op("scalar", lambda e, f=f, bnk=bnk: e.activation(
                                hid[:, f * NT:(f + 1) * NT], bnk[:, :], AF.Relu),
                                reads=[bkkey], writes=[f"hid{f % 4}"])
                    for dc in range(DCH):
                        slot = dc % 2
                        wsem = "w2a" if slot == 0 else "w2b"
                        pb.dma("sync", w2buf[:, slot * 4096:(slot + 1) * 4096],
                               sw2_d[ex, dc], wsem,
                               reads=[f"pp{ex}"], writes=[f"w2s{slot}"])
                        bnk = bank[dc % 4]
                        bkkey = f"b{dc % 4}"
                        for fc in range(NFC):
                            pb.op("tensor", lambda e, fc=fc, slot=slot, bnk=bnk:
                                  e.matmul(bnk[:, :],
                                           w2buf[:, slot * 4096 + fc * 128:slot * 4096 + (fc + 1) * 128],
                                           hid[:, fc * NT:(fc + 1) * NT],
                                           start=(fc == 0), stop=(fc == NFC - 1)),
                                  reads=[f"w2s{slot}"] + [f"hid{j}" for j in range(4)],
                                  writes=[bkkey] if fc == NFC - 1 else [],
                                  wait=(pb._dep_waits((), [bkkey]) if fc == 0 else []))
                        mslot = dc % 2
                        pb.op("vector", lambda e, bnk=bnk, mslot=mslot: e.tensor_tensor(
                            mtmp[:, mslot * NT:(mslot + 1) * NT], bnk[:, :],
                            cbs[:], ALU.mult),
                            reads=[bkkey, "cbs"], writes=[f"mtmp{mslot}"])
                        pb.op("vector", lambda e, dc=dc, t0=t0, mslot=mslot: e.tensor_tensor(
                            xch(dc)[:, t0:t0 + NT], xch(dc)[:, t0:t0 + NT],
                            mtmp[:, mslot * NT:(mslot + 1) * NT], ALU.add),
                            reads=[f"mtmp{mslot}", f"x{dc}"], writes=[f"x{dc}"])
            lctx.close()
            pb.fence()

        def emit_attn1(xn_done):
            yctx = ExitStack()
            yT = yctx.enter_context(nc.sbuf_tensor("yT1", [128, DCH * T], BF16))
            ych = lambda c: yT[:, c * T:(c + 1) * T]
            lctx = ExitStack()
            lsb = lambda name, shape, dt: lctx.enter_context(nc.sbuf_tensor(name, shape, dt))
            TA = lsb("TA1", [128, T], BF16)
            TB = lsb("TB1", [128, T], BF16)
            invfr = lsb("invfr1", [1, 128], F32)
            c11 = lsb("c11", [1, 1], F32)
            sgncol = lsb("sgncol", [128, 1], F32)
            kT1 = lsb("kT1", [128, 2 * T], BF16)
            qT1 = lsb("qT1", [128, T], BF16)
            vG = lsb("vG1", [128, 4 * 16 * 65], BF16)
            rtmpA = lsb("rtmpA", [128, NT], BF16)
            rtmpB = lsb("rtmpB", [128, NT], BF16)
            expb = lsb("exp1", [128, 4 * NT], BF16)
            rb2 = lsb("rb21", [128, 2 * NT], F32)
            CP = colpool
            xnr = [xn_done]
            actx = ExitStack()
            asb = lambda name, shape, dt: actx.enter_context(nc.sbuf_tensor(name, shape, dt))
            rsc = asb("rsc1", [128, 3 * NT], F32)
            rni = asb("rni1", [128, NT], I32)
            iota_t = asb("iota1", [1, NT], F32)



            pb.op("gpsimd", lambda e: e.iota(
                invfr[:], [[0, 4], [1, 32]], channel_multiplier=0,
                allow_small_or_imprecise_dtypes=True), writes=["invfr"])
            pb.op("vector", lambda e: e.memset(c11[:], 10000.0), writes=["c11"])
            pb.op("scalar", lambda e: e.activation(c11[:], c11[:], AF.Ln),
                  reads=["c11"], writes=["c11"])
            pb.op("vector", lambda e: e.tensor_scalar(
                invfr[:], invfr[:], float(1.0 / 32.0), None, ALU.mult),
                reads=["invfr"], writes=["invfr"])
            pb.op("vector", lambda e: e.tensor_scalar(
                invfr[:], invfr[:], c11[0:1, 0:1], None, ALU.mult),
                reads=["invfr", "c11"], writes=["invfr"])
            pb.op("scalar", lambda e: e.activation(
                invfr[:], invfr[:], AF.Exp, scale=-1.0),
                reads=["invfr"], writes=["invfr"])
            pb.op("vector", lambda e: e.memset(sgncol[0:32, :], -1.0), writes=["sgn"])
            pb.op("vector", lambda e: e.memset(sgncol[32:64, :], 1.0), writes=["sgn"])
            pb.op("vector", lambda e: e.memset(sgncol[64:96, :], -1.0), writes=["sgn"])
            pb.op("vector", lambda e: e.memset(sgncol[96:128, :], 1.0), writes=["sgn"])

            I2P_HI = float(np.float32(1.0 / (2 * np.pi)))
            I2P_LO = float(np.float64(1.0 / (2 * np.pi)) - np.float64(np.float32(1.0 / (2 * np.pi))))
            TWO_PI_HI = float(np.float32(2 * np.pi))
            TWO_PI_LO = float(np.float64(2 * np.pi) - np.float64(np.float32(2 * np.pi)))
            PC = float(np.float32(3.1415925))
            r_ang = rsc[:, 0:NT]
            r_y = rsc[:, NT:2 * NT]
            r_nf = rsc[:, 2 * NT:3 * NT]

            def reduce_sin(dst, dkey, pre_add, signed):
                if pre_add != 0.0:
                    pb.op("vector", lambda e: e.tensor_scalar(
                        r_ang[:, :], r_ang[:, :], pre_add, None, ALU.add),
                        reads=["r_ang"], writes=["r_ang"])
                pb.op("vector", lambda e: e.tensor_scalar(
                    r_y[:, :], r_ang[:, :], I2P_LO, None, ALU.mult),
                    reads=["r_ang"], writes=["r_y"])
                pb.op("vector", lambda e: e.scalar_tensor_tensor(
                    r_y[:, :], r_ang[:, :], I2P_HI, r_y[:, :], ALU.mult, ALU.add),
                    reads=["r_ang", "r_y"], writes=["r_y"])
                pb.op("vector", lambda e: e.tensor_copy(rni[:, :], r_y[:, :]),
                      reads=["r_y"], writes=["rni"])
                pb.op("vector", lambda e: e.tensor_copy(r_nf[:, :], rni[:, :]),
                      reads=["rni"], writes=["r_nf"])
                pb.op("vector", lambda e: e.tensor_tensor(
                    r_y[:, :], r_y[:, :], r_nf[:, :], ALU.subtract),
                    reads=["r_y", "r_nf"], writes=["r_y"])
                pb.op("vector", lambda e: e.tensor_scalar(
                    r_y[:, :], r_y[:, :], 0.5, None, ALU.is_ge),
                    reads=["r_y"], writes=["r_y"])
                pb.op("vector", lambda e: e.tensor_tensor(
                    r_nf[:, :], r_nf[:, :], r_y[:, :], ALU.add),
                    reads=["r_nf", "r_y"], writes=["r_nf"])
                pb.op("vector", lambda e: e.scalar_tensor_tensor(
                    r_ang[:, :], r_nf[:, :], -TWO_PI_HI, r_ang[:, :], ALU.mult, ALU.add),
                    reads=["r_nf", "r_ang"], writes=["r_ang"])
                pb.op("vector", lambda e: e.scalar_tensor_tensor(
                    r_ang[:, :], r_nf[:, :], -TWO_PI_LO, r_ang[:, :], ALU.mult, ALU.add),
                    reads=["r_nf", "r_ang"], writes=["r_ang"])
                pb.op("vector", lambda e: e.tensor_scalar(
                    r_ang[:, :], r_ang[:, :], -PC, PC, ALU.max, ALU.min),
                    reads=["r_ang"], writes=["r_ang"])
                if signed:
                    pb.op("scalar", lambda e: e.activation(
                        r_y[:, :], r_ang[:, :], AF.Sin),
                        reads=["r_ang"], writes=["r_y"])
                    pb.op("vector", lambda e, dst=dst: e.tensor_scalar(
                        dst, r_y[:, :], sgncol[:, :], None, ALU.mult),
                        reads=["r_y", "sgn"], writes=[dkey])
                else:
                    pb.op("scalar", lambda e, dst=dst: e.activation(
                        dst, r_ang[:, :], AF.Sin),
                        reads=["r_ang"], writes=[dkey])

            for tau in range(NTAU):
                t0 = tau * NT
                pb.op("gpsimd", lambda e, t0=t0: e.iota(
                    iota_t[:], [[1, NT]], base=t0, channel_multiplier=0,
                    allow_small_or_imprecise_dtypes=True), writes=["iota_t"])
                pb.op("tensor", lambda e, t0=t0: e.matmul(
                    bank[7][:, :], invfr[0:1, :], iota_t[0:1, 0:NT],
                    start=True, stop=True),
                    reads=["invfr", "iota_t"], writes=["b7"])
                pb.op("vector", lambda e: e.tensor_copy(r_ang[:, :], bank[7][:, :]),
                      reads=["b7"], writes=["r_ang", "angsave"])
                reduce_sin(TA[:, t0:t0 + NT], "TA", float(np.pi / 2), False)
                pb.op("vector", lambda e: e.tensor_copy(r_ang[:, :], bank[7][:, :]),
                      reads=["b7", "angsave"], writes=["r_ang"])
                reduce_sin(TB[:, t0:t0 + NT], "TB", 0.0, True)

            actx.close()
            pb.fence()

            bctx = ExitStack()
            wkvbf = bctx.enter_context(nc.sbuf_tensor("wkvbf", [128, 2 * DCH * 256], BF16))
            wk_r = wk1_d.rearrange("(c p) n -> c p n", p=128)
            wv_r = wv1_d.rearrange("(c p) n -> c p n", p=128)
            load_cast(wkvbf, lambda i: wk_r[i], DCH, 256, "wk1")
            load_cast(wkvbf[:, DCH * 256:], lambda i: wv_r[i], DCH, 256, "wv1")

            def rope_evac(bnk, bkkey, dst, dcol0, dkey, t0):
                pb.op("scalar", lambda e, bnk=bnk: e.copy(rtmpA[:, :], bnk[:, :]),
                      reads=[bkkey], writes=["rtA"])
                for gidx, (d0, s0) in enumerate(((0, 32), (32, 0), (64, 96), (96, 64))):
                    pb.op("scalar", lambda e, bnk=bnk, d0=d0, s0=s0: e.copy(
                        rtmpB[d0:d0 + 32, :], bnk[s0:s0 + 32, :]),
                        reads=[bkkey], writes=["rtB"] if gidx == 3 else [],
                        wait=(pb._dep_waits((), ["rtB"]) if gidx == 0 else []))
                pb.op("vector", lambda e, t0=t0: e.tensor_tensor(
                    rtmpA[:, :], rtmpA[:, :], TA[:, t0:t0 + NT], ALU.mult),
                    reads=["rtA", "TA"], writes=["rtA"])
                pb.op("vector", lambda e, t0=t0: e.tensor_tensor(
                    rtmpB[:, :], rtmpB[:, :], TB[:, t0:t0 + NT], ALU.mult),
                    reads=["rtB", "TB"], writes=["rtB"])
                pb.op("vector", lambda e, dst=dst, t0=t0, dcol0=dcol0: e.tensor_tensor(
                    dst[:, dcol0 + t0:dcol0 + t0 + NT], rtmpA[:, :], rtmpB[:, :], ALU.add),
                    reads=["rtA", "rtB"], writes=[dkey])

            for g in range(4):
                for j in range(16):
                    pb.op("vector", lambda e, g=g, j=j: e.memset(
                        vG[:, g * 1040 + j * 65 + 64: g * 1040 + j * 65 + 65], 1.0),
                        writes=["vGones"])
            for kvc in range(2):
                for tau in range(NTAU):
                    t0 = tau * NT
                    pj = 6 + tau % 2
                    for dc in range(DCH):
                        pb.op("tensor", lambda e, dc=dc, kvc=kvc, t0=t0, pj=pj: e.matmul(
                            bank[pj][:, :],
                            wkvbf[:, dc * 256 + kvc * 128: dc * 256 + (kvc + 1) * 128],
                            xnch(dc)[:, t0:t0 + NT],
                            start=(dc == 0), stop=(dc == DCH - 1)),
                            reads=[f"wk1{dc}", f"xn{dc}.{tau}"],
                            writes=[f"b{pj}"] if dc == DCH - 1 else [],
                            wait=(pb._dep_waits((), [f"b{pj}"]) if dc == 0 else []) + xnr)
                    rope_evac(bank[pj], f"b{pj}", kT1, kvc * T, f"kT1.{kvc}", t0)
                for mt in range(16):
                    bnk = bank[6 + mt % 2]
                    bkkey = f"b{6 + mt % 2}"
                    for dc in range(DCH):
                        pb.op("tensor", lambda e, dc=dc, mt=mt, kvc=kvc, bnk=bnk: e.matmul(
                            bnk[:, 0:128],
                            xnch(dc)[:, mt * 128:(mt + 1) * 128],
                            wkvbf[:, DCH * 256 + dc * 256 + kvc * 128:
                                  DCH * 256 + dc * 256 + (kvc + 1) * 128],
                            start=(dc == 0), stop=(dc == DCH - 1)),
                            reads=[f"wv1{dc}", f"xn{dc}.{mt // 4}"],
                            writes=[bkkey] if dc == DCH - 1 else [],
                            wait=(pb._dep_waits((), [bkkey]) if dc == 0 else []) + xnr)
                    for gi in range(2):
                        g = 2 * kvc + gi
                        pb.op("scalar", lambda e, mt=mt, g=g, gi=gi, bnk=bnk: e.copy(
                            vG[:, g * 1040 + mt * 65: g * 1040 + mt * 65 + 64],
                            bnk[:, gi * 64:(gi + 1) * 64]),
                            reads=[bkkey, "vGones"], writes=[f"vG{g}"])
            for kvc in range(2):
                pb.op("vector", lambda e, kvc=kvc: e.tensor_tensor(
                    expb[:, 0:T], kT1[:, kvc * T:(kvc + 1) * T],
                    kT1[:, kvc * T:(kvc + 1) * T], ALU.mult),
                    reads=[f"kT1.{kvc}"], writes=["e0", "e1", "e2", "e3"])
                for tau in range(NTAU):
                    pb.op("tensor", lambda e, tau=tau: e.matmul(
                        bank[4][0:1, :], onescol[:, :],
                        expb[:, tau * NT:(tau + 1) * NT], start=True, stop=True),
                        reads=[f"e{tau}", "onescol"], writes=["b4"])
                    pb.op("vector", lambda e, tau=tau: e.reduce_max(
                        CP[0:1, tau:tau + 1], bank[4][0:1, :], AX.X),
                        reads=["b4"], writes=[f"cp{tau}"])
                pb.op("vector", lambda e, kvc=kvc: e.reduce_max(
                    CP[0:1, 16 + kvc:17 + kvc], CP[0:1, 0:4], AX.X),
                    reads=[f"cp{i}" for i in range(4)], writes=[f"knc{kvc}"])

            bctx.close()
            pb.fence()
            cmask = lsb("cmaskS", [128, 4 * NT], BF16)
            wq1bf = lsb("wq1bf", [128, DCH * 128], BF16)
            rrow = lsb("rrow1", [1, 2 * NT], F32)
            rhl1 = lsb("rhl1", [1, 4 * NT], BF16)
            pb.dma("sync", cmask[:, :], cmask_d[:, :], "dmisc", writes=["cmask"])
            PAIR_ORDER = [0, 4, 1, 5, 2, 6, 3, 7, 8, 12, 9, 13, 10, 14, 11, 15]
            wq_r = wq1_d.rearrange("(c p) n -> p c n", p=128)
            for hp in range(DCH):
                pb.dma("sync", stg[:, 0:1024].rearrange("p (c n) -> p c n", n=128),
                       wq_r[:, :, hp * 128:(hp + 1) * 128], "stga",
                       writes=["stg0"])
                pb.op("vector", lambda e: e.tensor_scalar(
                    wq1bf[:, :], stg[:, 0:1024], float(1.0 / 8.0), None, ALU.mult),
                    reads=["stg0"], writes=["wq1bf"])
                for tau in range(NTAU):
                    t0 = tau * NT
                    pj = 6 + tau % 2
                    for dc in range(DCH):
                        pb.op("tensor", lambda e, dc=dc, t0=t0, pj=pj: e.matmul(
                            bank[pj][:, :], wq1bf[:, dc * 128:(dc + 1) * 128],
                            xnch(dc)[:, t0:t0 + NT],
                            start=(dc == 0), stop=(dc == DCH - 1)),
                            reads=["wq1bf", f"xn{dc}.{tau}"],
                            writes=[f"b{pj}"] if dc == DCH - 1 else [],
                            wait=(pb._dep_waits((), [f"b{pj}"]) if dc == 0 else []) + xnr)
                    rope_evac(bank[pj], f"b{pj}", qT1, 0, "qT1", t0)
                pb.op("vector", lambda e: e.tensor_tensor(
                    expb[:, 0:T], qT1[:, :], qT1[:, :], ALU.mult),
                    reads=["qT1"], writes=["e0", "e1", "e2", "e3"])
                for tau in range(NTAU):
                    pb.op("tensor", lambda e, tau=tau: e.matmul(
                        bank[4][0:1, :], onescol[:, :],
                        expb[:, tau * NT:(tau + 1) * NT], start=True, stop=True),
                        reads=[f"e{tau}", "onescol"], writes=["b4"])
                    pb.op("vector", lambda e, tau=tau: e.reduce_max(
                        CP[0:1, tau:tau + 1], bank[4][0:1, :], AX.X),
                        reads=["b4"], writes=[f"cp{tau}"])
                pb.op("vector", lambda e: e.reduce_max(CP[0:1, 8:9], CP[0:1, 0:4], AX.X),
                      reads=[f"cp{i}" for i in range(4)], writes=["cpq"])
                kchunk = hp // 4
                pb.op("vector", lambda e, kchunk=kchunk: e.tensor_tensor(
                    CP[0:1, 10:11], CP[0:1, 8:9], CP[0:1, 16 + kchunk:17 + kchunk],
                    ALU.mult),
                    reads=["cpq", f"knc{kchunk}"], writes=["cpm"])
                pb.op("scalar", lambda e: e.activation(
                    CP[0:1, 10:11], CP[0:1, 10:11], AF.Ln),
                    reads=["cpm"], writes=["cpm"])
                pb.op("scalar", lambda e: e.activation(
                    CP[0:1, 10:11], CP[0:1, 10:11], AF.Exp, scale=0.5),
                    reads=["cpm"], writes=["cpm"])
                pb.op("vector", lambda e: e.tensor_scalar(
                    CP[0:1, 11:12], CP[0:1, 10:11], -1.0, None, ALU.mult),
                    reads=["cpm"], writes=["cpn"])
                pb.op("tensor", lambda e: e.matmul(
                    bank[4][:, 0:1], onesf[:, :], CP[0:1, 11:12],
                    start=True, stop=True),
                    reads=["cpn", "onesf"], writes=["b4"])
                pb.op("vector", lambda e: e.tensor_copy(CP[:, 13:14], bank[4][:, 0:1]),
                      reads=["b4"], writes=["neg1"])

                def mk_fin1(h12, ytgt, ybase, t0):
                    def fin():
                        avb = 4 + h12
                        bb = 6 + h12
                        hi = rhl1[0:1, (2 * h12) * NT:(2 * h12 + 1) * NT]
                        lo = rhl1[0:1, (2 * h12 + 1) * NT:(2 * h12 + 2) * NT]
                        rr = rrow[0:1, h12 * NT:(h12 + 1) * NT]
                        pb.op("scalar", lambda e: e.activation(
                            rr, bank[avb][64:65, :], AF.Ln),
                            reads=[f"b{avb}"], writes=[f"rrow{h12}"])
                        pb.op("scalar", lambda e: e.activation(
                            rr, rr, AF.Exp, scale=-1.0),
                            reads=[f"rrow{h12}"], writes=[f"rrow{h12}"])
                        pb.op("vector", lambda e: e.tensor_copy(hi, rr),
                              reads=[f"rrow{h12}"], writes=[f"rhi{h12}"])
                        pb.op("vector", lambda e: e.tensor_tensor(lo, rr, hi, ALU.subtract),
                              reads=[f"rrow{h12}", f"rhi{h12}"], writes=[f"rlo{h12}"])
                        pb.op("tensor", lambda e: e.matmul(
                            bank[bb][:, :], onesb[0:1, 0:128], hi, start=True, stop=False),
                            reads=[f"rhi{h12}", "onesb"],
                            wait=pb._dep_waits((), [f"b{bb}"]))
                        pb.op("tensor", lambda e: e.matmul(
                            bank[bb][:, :], onesb[0:1, 0:128], lo, start=False, stop=True),
                            reads=[f"rlo{h12}", "onesb"], writes=[f"b{bb}"])
                        rbs = rb2[:, h12 * NT:(h12 + 1) * NT]
                        pb.op("vector", lambda e: e.tensor_copy(rbs, bank[bb][:, :]),
                              reads=[f"b{bb}"], writes=[f"rb{h12}"])
                        pb.op("vector", lambda e, avb=avb, rbs=rbs: e.tensor_tensor(
                            ych(ytgt)[ybase:ybase + 64, t0:t0 + NT],
                            bank[avb][0:64, :], rbs[0:64, :], ALU.mult),
                            reads=[f"b{avb}", f"rb{h12}"], writes=[f"y{ytgt}"])
                    return fin

                hA = PAIR_ORDER[2 * hp]
                hB = PAIR_ORDER[2 * hp + 1]
                gA, gB = hA // 4, hB // 4
                assert (gA % 2, gB % 2) == (0, 1) and gA // 2 == kchunk and gB // 2 == kchunk
                tiles = []
                for tau in range(NTAU):
                    t0 = tau * NT
                    tiles.append(dict(
                        nkc=4 * tau + 4,
                        klhsT=(lambda kc, h12, kchunk=kchunk:
                               kT1[64 * h12:64 * h12 + 64,
                                   kchunk * T + kc * 128: kchunk * T + (kc + 1) * 128]),
                        qrhs=[qT1[0:64, t0:t0 + NT], qT1[64:128, t0:t0 + NT]],
                        kreads=[f"kT1.{kchunk}"], qkey="qT1",
                        vap=(lambda kc, h12, gA=gA, gB=gB:
                             vG[:, (gA if h12 == 0 else gB) * 1040 + kc * 65:
                                (gA if h12 == 0 else gB) * 1040 + (kc + 1) * 65]),
                        vkeys=[f"vG{gA}", f"vG{gB}"], expb=expb, cmask=cmask,
                        mask_from=4 * tau,
                        negcol=CP[:, 13:14], ckey="neg1",
                        fins=[mk_fin1(0, hA // 2, (hA % 2) * 64, t0),
                              mk_fin1(1, hB // 2, (hB % 2) * 64, t0)],
                    ))
                attn_stream(tiles)
            lctx.close()
            pb.fence()
            wctx = ExitStack()
            wsb = lambda name, shape, dt: wctx.enter_context(nc.sbuf_tensor(name, shape, dt))
            wobf = wsb("wobf", [128, DCH * 1024], BF16)
            wtm = wsb("wtmp1", [128, 2 * NT], F32)
            wo_r = wo1_d.rearrange("(c p) n -> c p n", p=128)
            load_cast(wobf, lambda i: wo_r[i], DCH, 1024, "wo1")
            wo_units = []
            for dc in range(DCH):
                for tau in range(NTAU):
                    def wo_unit(dc=dc, tau=tau):
                        t0 = tau * NT
                        bnk = bank[tau % 2]
                        bkkey = f"b{tau % 2}"
                        wslot = tau % 2
                        for yc in range(DCH):
                            pb.op("tensor", lambda e, yc=yc, dc=dc, t0=t0, bnk=bnk: e.matmul(
                                bnk[:, :],
                                wobf[:, yc * 1024 + dc * 128: yc * 1024 + (dc + 1) * 128],
                                ych(yc)[:, t0:t0 + NT],
                                start=(yc == 0), stop=(yc == DCH - 1)),
                                reads=[f"wo1{yc}", f"y{yc}"],
                                writes=[bkkey] if yc == DCH - 1 else [],
                                wait=(pb._dep_waits((), [bkkey]) if yc == 0 else []))
                        pb.op("vector", lambda e, bnk=bnk, wslot=wslot: e.tensor_copy(
                            wtm[:, wslot * NT:(wslot + 1) * NT], bnk[:, :]),
                            reads=[bkkey], writes=[f"wtm{wslot}"])
                        pb.op("vector", lambda e, dc=dc, t0=t0, wslot=wslot: e.tensor_tensor(
                            xch(dc)[:, t0:t0 + NT], xch(dc)[:, t0:t0 + NT],
                            wtm[:, wslot * NT:(wslot + 1) * NT], ALU.add),
                            reads=[f"wtm{wslot}", f"x{dc}"], writes=[f"x{dc}"])
                    wo_units.append(wo_unit)
            return (yctx, wctx), wo_units

        def emit_ffn1(xn_done, wo_units=(), wctx=None):
            lctx = ExitStack()
            lsb = lambda name, shape, dt: lctx.enter_context(nc.sbuf_tensor(name, shape, dt))
            wtf = lsb("wtf", [128, 2 * 1024], F32)
            wabs = lsb("wabs", [128, 512], F32)
            wmask = lsb("wmask", [128, 1024], BF16)
            wsgn = lsb("wsgn", [128, 1024], BF16)
            wtb = lsb("wtb", [128, DCH * 1024], BF16)
            bi8 = lsb("bi8", [NEXP, 128], F32)
            bcolT = lsb("bcolT", [128, 8], F32)
            frow = lsb("frow", [1, 16], F32)
            onescf = lsb("onescf", [128, 1], F32)
            ftm = lsb("ftm", [128, 2 * NT], F32)
            idf32x = lsb("idf32x", [128, 128], F32)
            CP = colpool
            xnr = [xn_done]
            wt_r = wtT_d.rearrange("(c p) n -> c p n", p=128)
            for dc in range(DCH):
                sl = dc % 2
                pb.dma("sync", wtf[:, sl * 1024:(sl + 1) * 1024], wt_r[dc], "dmisc",
                       writes=[f"wtf{sl}"])
                pb.op("vector", lambda e, dc=dc, sl=sl: e.reduce_sum(
                    CP[:, 20 + dc:21 + dc], wtf[:, sl * 1024:(sl + 1) * 1024],
                    AX.X, apply_absolute_value=True),
                    reads=[f"wtf{sl}"], writes=[f"cs{dc}"])
            pb.dma("sync", bi8[:, :], bi1_d.rearrange("x (c n) -> (x c) n", n=128), "dmisc",
                   writes=["bi8"])
            pb.dma("sync", frow[0:1, 0:1], al1_d[0:1, 0:1], "dmisc", writes=["alpha"])
            pb.dma("sync", idf32x[:, :], idf32_d[:, :], "dmisc", writes=["idf32x"])
            pb.op("vector", lambda e: e.memset(onescf[:], 1.0), writes=["onescf"])
            pb.op("tensor", lambda e: e.matmul(
                bank[4][0:1, 0:8], onescf[:, :], CP[:, 20:28],
                start=True, stop=True),
                reads=["onescf"] + [f"cs{i}" for i in range(8)], writes=["b4"])
            pb.op("vector", lambda e: e.tensor_copy(frow[0:1, 8:16], bank[4][0:1, 0:8]),
                  reads=["b4"], writes=["frow8"])
            pb.op("vector", lambda e: e.reduce_sum(
                frow[0:1, 1:2], frow[0:1, 8:16], AX.X),
                reads=["frow8"], writes=["dsum"])
            pb.op("vector", lambda e: e.tensor_scalar(
                frow[0:1, 1:2], frow[0:1, 1:2], float(0.7 / (D * D)), None, ALU.mult),
                reads=["dsum"], writes=["dsum"])
            for (src_sl, cpcol, kk) in ((frow[0:1, 1:2], 28, "deltacol"),
                                        (frow[0:1, 0:1], 29, "alphacol")):
                pb.op("tensor", lambda e, src_sl=src_sl: e.matmul(
                    bank[4][:, 0:1], onesf[:, :], src_sl, start=True, stop=True),
                    reads=["dsum", "alpha", "onesf"], writes=["b4"])
                pb.op("vector", lambda e, cpcol=cpcol: e.tensor_copy(
                    CP[:, cpcol:cpcol + 1], bank[4][:, 0:1]),
                    reads=["b4"], writes=[kk])
            pb.op("tensor", lambda e: e.transpose(
                bank[4][:, 0:8], bi8[0:NEXP, :], idf32x[0:NEXP, 0:NEXP]),
                reads=["bi8", "idf32x"], writes=["b4"])
            pb.op("vector", lambda e: e.tensor_copy(bcolT[:, :], bank[4][:, 0:8]),
                  reads=["b4"], writes=["bcolT"])
            for dc in range(DCH):
                sl = dc % 2
                pb.dma("sync", wtf[:, sl * 1024:(sl + 1) * 1024], wt_r[dc], "dmisc",
                       writes=[f"wtf{sl}"])
                pb.op("scalar", lambda e, dc=dc, sl=sl: e.sign(
                    wsgn[:, :], wtf[:, sl * 1024:(sl + 1) * 1024]),
                    reads=[f"wtf{sl}"], writes=["wsgn"])
                for hh in range(2):
                    pb.op("scalar", lambda e, sl=sl, hh=hh: e.activation(
                        wabs[:, :], wtf[:, sl * 1024 + hh * 512: sl * 1024 + (hh + 1) * 512],
                        AF.Abs),
                        reads=[f"wtf{sl}"], writes=["wabs"])
                    pb.op("vector", lambda e, hh=hh: e.tensor_scalar(
                        wmask[:, hh * 512:(hh + 1) * 512], wabs[:, :],
                        CP[:, 28:29], None, ALU.is_gt),
                        reads=["wabs", "deltacol"], writes=["wmask"])
                pb.op("vector", lambda e, dc=dc: e.tensor_tensor(
                    wtb[:, dc * 1024:(dc + 1) * 1024], wsgn[:, :], wmask[:, :],
                    ALU.mult),
                    reads=["wsgn", "wmask"], writes=[f"wtb{dc}"])
            wo_units = list(wo_units)
            ui = 0
            for oc in range(DCH):
                for tau in range(NTAU):
                    t0 = tau * NT
                    bnk = bank[2 + tau % 2]
                    bkkey = f"b{2 + tau % 2}"
                    for dc in range(DCH):
                        pb.op("tensor", lambda e, dc=dc, oc=oc, t0=t0, bnk=bnk: e.matmul(
                            bnk[:, :],
                            wtb[:, dc * 1024 + oc * 128: dc * 1024 + (oc + 1) * 128],
                            xnch(dc)[:, t0:t0 + NT],
                            start=(dc == 0), stop=(dc == DCH - 1)),
                            reads=[f"wtb{dc}", f"xn{dc}.{tau}"],
                            writes=[bkkey] if dc == DCH - 1 else [],
                            wait=(pb._dep_waits((), [bkkey]) if dc == 0 else []) + xnr)
                    fslot = tau % 2
                    pb.op("vector", lambda e, bnk=bnk, oc=oc, fslot=fslot: e.tensor_scalar(
                        ftm[:, fslot * NT:(fslot + 1) * NT], bnk[:, :],
                        CP[:, 29:30], bcolT[:, oc:oc + 1], ALU.mult, ALU.add),
                        reads=[bkkey, "alphacol", "bcolT"], writes=[f"ftm{fslot}"])
                    pb.op("vector", lambda e, oc=oc, t0=t0, fslot=fslot: e.tensor_tensor(
                        xch(oc)[:, t0:t0 + NT], xch(oc)[:, t0:t0 + NT],
                        ftm[:, fslot * NT:(fslot + 1) * NT], ALU.add),
                        reads=[f"ftm{fslot}", f"x{oc}"], writes=[f"x{oc}"])
                    if ui < len(wo_units):
                        wo_units[ui]()
                        ui += 1
            while ui < len(wo_units):
                wo_units[ui]()
                ui += 1
            lctx.close()
            if wctx is not None:
                wctx[1].close()
                wctx[0].close()
            pb.fence()

        xT_ready = [din_inputs]

        if "ln0" in stages:
            xn_done = emit_layernorm("ln0", xT_ready)
        else:
            xn_done = ("dv", 0)

        if "attn0" in stages:
            emit_attn0(xn_done)

        if "moe0" in stages:
            emit_moe0(xn_done)

        if "ln1" in stages:
            xn1_done = emit_layernorm("ln1", [("dv", pb.cnt["dv"])])
            wo_ctx, wo_units = (None, ())
            if "attn1" in stages:
                wo_ctx, wo_units = emit_attn1(xn1_done)
            if "ffn1" in stages:
                emit_ffn1(xn1_done, wo_units, wo_ctx)
            elif wo_ctx is not None:
                for u in wo_units:
                    u()
                wo_ctx[1].close()
                wo_ctx[0].close()
                pb.fence()

        if dbg == "xn0":
            for dc in range(DCH):
                pb.op("vector", lambda e, dc=dc: e.tensor_copy(xch(dc), xnch(dc)),
                      reads=[f"xn{dc}.{tau}" for tau in range(NTAU)],
                      writes=[f"x{dc}"], wait=[xn_done])

        final_waits = [("dv", pb.cnt["dv"]), ("sc", pb.cnt["sc"]),
                       ("pe", pb.cnt["pe"]), ("gp", pb.cnt["gp"])]
        for dc in range(DCH):
            pb.dma("gpsimd", out_t[dc], xch(dc), "dst",
                   reads=[f"x{dc}"], wait=final_waits)
        pb.on("gpsimd", lambda e: e.engine_nop(), wait=[("dst", pb.cnt["dst"])])

        pb.emit()
    return nc


def _prep_core_inputs(inputs):
    x = np.ascontiguousarray(inputs["L0_attn_x"])
    perm64 = np.concatenate([np.arange(0, 64, 2), np.arange(1, 64, 2)])
    PAIR_ORDER = [0, 4, 1, 5, 2, 6, 3, 7, 8, 12, 9, 13, 10, 14, 11, 15]
    permq = np.concatenate([PAIR_ORDER[i] * 64 + perm64 for i in range(NH)])
    permk = np.concatenate([h * 64 + perm64 for h in range(NKV)])

    wq1 = np.ascontiguousarray(inputs["L1_attn_Wq"][:, permq])
    wk1 = np.ascontiguousarray(inputs["L1_attn_Wk"][:, permk])
    wtT = np.ascontiguousarray(inputs["L1_ffn_weight"].T)

    sel = np.zeros((NEXP, NEXP * 128), np.float32)
    for e in range(NEXP):
        sel[e, e * 128:(e + 1) * 128] = 1.0
    hmask = np.zeros((128, 2), BF)
    hmask[0:64, 0] = 1
    hmask[64:128, 1] = 1
    cmask = np.zeros((128, 4 * NT), BF)
    for o in range(4):
        j = np.arange(128)[:, None]
        i = np.arange(NT)[None, :]
        cmask[:, o * NT:(o + 1) * NT] = (i >= 128 * o + j).astype(BF)

    shared = {
        "l0wq": np.ascontiguousarray(inputs["L0_attn_Wq"]),
        "l0wk": np.ascontiguousarray(inputs["L0_attn_Wk"]),
        "l0bq": np.ascontiguousarray(inputs["L0_attn_bq"]).reshape(1, D),
        "l0bk": np.ascontiguousarray(inputs["L0_attn_bk"]).reshape(1, D),
        "l0wr": np.ascontiguousarray(inputs["L0_ffn_Wr"]),
        "l0w1": np.ascontiguousarray(inputs["L0_ffn_W1"]),
        "l0w2": np.ascontiguousarray(inputs["L0_ffn_W2"]),
        "l1wq": wq1, "l1wk": wk1,
        "l1wv": np.ascontiguousarray(inputs["L1_attn_Wv"]),
        "l1wo": np.ascontiguousarray(inputs["L1_attn_Wo"]),
        "l1wt": wtT,
        "l1alpha": np.ascontiguousarray(inputs["L1_ffn_alpha"]).reshape(1, 1),
        "l1bias": np.ascontiguousarray(inputs["L1_ffn_bias"]).reshape(1, D),
        "sel8": sel, "hmask": hmask, "cmask": cmask,
        "idf32c": np.eye(128, dtype=np.float32),
        "sel2c": np.concatenate([
            np.concatenate([np.ones((1, 128), np.float32), np.zeros((1, 128), np.float32)], 1),
            np.concatenate([np.zeros((1, 128), np.float32), np.ones((1, 128), np.float32)], 1)], 0),
    }
    in_maps = []
    for c in range(NCORES):
        m = dict(shared)
        m["xT"] = np.ascontiguousarray(x[c].T)
        in_maps.append(m)
    return in_maps


def kernel(**inputs):
    in_maps = _prep_core_inputs(inputs)
    nc = build_nc(ALL_STAGES)
    res = run_bass_kernel_spmd(nc, in_maps, core_ids=list(range(NCORES)))
    out = np.empty((NCORES, T, D), np.float32)
    for c in range(NCORES):
        out[c] = res.results[c]["outT"].T
    return out


# version 27
# speedup vs baseline: 1.0187x; 1.0187x over previous
import numpy as np
import ml_dtypes
import concourse.bass as bass
import concourse.mybir as mybir
from concourse.bass_utils import run_bass_kernel_spmd
from contextlib import ExitStack

F32 = mybir.dt.float32
BF16 = mybir.dt.bfloat16
I32 = mybir.dt.int32
AF = mybir.ActivationFunctionType
ALU = mybir.AluOpType
AX = mybir.AxisListType
BF = ml_dtypes.bfloat16

D = 1024
T = 2048
NCORES = 8
DCH = 8
NTAU = 4
NT = 512
NEXP = 8
FF = 4096
NFC = 32
NH = 16
HDIM = 64
NKV = 4
PI = float(np.pi)

ALL_STAGES = ("ln0", "attn0", "moe0", "ln1", "attn1", "ffn1")


class PB:
    ENGINES = ("tensor", "vector", "scalar", "gpsimd", "sync")

    def __init__(self, nc, ctx):
        self.nc, self.ctx = nc, ctx
        self.q = {e: [] for e in self.ENGINES}
        self.sems = {}
        self.cnt = {}
        self.waited = {e: {} for e in self.ENGINES}

    def sem(self, name):
        h = self.ctx.enter_context(self.nc.semaphore(name))
        self.sems[name] = h
        self.cnt[name] = 0
        return h

    def tick(self, name, n=1):
        self.cnt[name] += n
        return self.cnt[name]

    def fence(self, exempt=()):
        snap = [(s, self.cnt[s]) for s in self.cnt]
        self.last_fence = snap
        self.fence_exempt = set(exempt)
        if hasattr(self, "ready"):
            for k, st in self.ready.items():
                if k not in exempt:
                    st["fence"] = snap

    def on(self, eng, fn, wait=(), inc=()):
        if getattr(self, "pending", None) and self.pending.get(eng):
            wait = self.pending.pop(eng) + list(wait)
        w = []
        for s, c in wait:
            if c is None or c <= 0:
                continue
            if self.waited[eng].get(s, 0) < c:
                self.waited[eng][s] = c
                w.append((s, c))
        self.q[eng].append((w, fn, list(inc)))

    def _dep_waits(self, reads, writes):
        if not hasattr(self, "ready"):
            self.ready = {}
        w = []
        for k in reads:
            st = self.ready.get(k)
            if st and st["w"]:
                w.append(st["w"])
        for k in writes:
            st = self.ready.get(k)
            if st:
                if st["w"]:
                    w.append(st["w"])
                w.extend(st["r"])
                if st.get("fence"):
                    w.extend(st.pop("fence"))
            elif getattr(self, "last_fence", None) and k not in getattr(self, "fence_exempt", ()):
                w.extend(self.last_fence)
        return w

    def _dep_update(self, reads, writes, tok):
        if not hasattr(self, "ready"):
            self.ready = {}
        for k in reads:
            st = self.ready.setdefault(k, {"w": None, "r": []})
            st["r"].append(tok)
        for k in writes:
            self.ready[k] = {"w": tok, "r": []}

    def dma(self, qeng, out, in_, sem, reads=(), writes=(), wait=(), serialize=None):
        if serialize is None:
            serialize = sem == "dmisc"
        w = list(wait) + self._dep_waits(reads, writes)
        if serialize and self.cnt[sem] > 0:
            w = [(sem, self.cnt[sem])] + w
        c = self.tick(sem, 16)
        self.on(qeng, lambda e, o=out, i=in_: e.dma_start(o, i), wait=w, inc=[(sem, 16)])
        self._dep_update(reads, writes, (sem, c))
        return (sem, c)

    def op(self, eng, fn, reads=(), writes=(), wait=()):
        sem = {"tensor": "pe", "vector": "dv", "scalar": "sc", "gpsimd": "gp"}[eng]
        w = list(wait) + self._dep_waits(reads, writes)
        c = self.tick(sem, 1)
        self.on(eng, fn, wait=w, inc=[(sem, 1)])
        self._dep_update(reads, writes, (sem, c))
        return (sem, c)

    def opq(self, eng, fn, reads=(), writes=(), wait=()):
        w = list(wait) + self._dep_waits(reads, writes)
        self.on(eng, fn, wait=w)

    def emit(self):
        blk = self.ctx.enter_context(self.nc.Block())
        sems = self.sems

        def mk(ops):
            def body(eng):
                for w, fn, incs in ops:
                    for s, c in w:
                        eng.wait_ge(sems[s], c)
                    ins = fn(eng)
                    for s, n in incs:
                        ins.then_inc(sems[s], n)
            return body

        blk.tensor(mk(self.q["tensor"]))
        blk.vector(mk(self.q["vector"]))
        blk.scalar(mk(self.q["scalar"]))
        blk.gpsimd(mk(self.q["gpsimd"]))
        blk.sync(mk(self.q["sync"]))


def ceil_div(a, b):
    return (a + b - 1) // b


def build_nc(stages=ALL_STAGES, dbg=None):
    nc = bass.Bass("TRN2", target_bir_lowering=False)

    xT_d = nc.dram_tensor("xT", [D, T], F32, kind="ExternalInput").ap()
    wq0_d = nc.dram_tensor("l0wq", [D, D], F32, kind="ExternalInput").ap()
    wk0_d = nc.dram_tensor("l0wk", [D, D], F32, kind="ExternalInput").ap()
    bq0_d = nc.dram_tensor("l0bq", [1, D], F32, kind="ExternalInput").ap()
    bk0_d = nc.dram_tensor("l0bk", [1, D], F32, kind="ExternalInput").ap()
    wr0_d = nc.dram_tensor("l0wr", [D, NEXP], F32, kind="ExternalInput").ap()
    w1_d = nc.dram_tensor("l0w1", [NEXP, D, FF], F32, kind="ExternalInput").ap()
    w2_d = nc.dram_tensor("l0w2", [NEXP, FF, D], F32, kind="ExternalInput").ap()
    wq1_d = nc.dram_tensor("l1wq", [D, D], F32, kind="ExternalInput").ap()
    wk1_d = nc.dram_tensor("l1wk", [D, 256], F32, kind="ExternalInput").ap()
    wv1_d = nc.dram_tensor("l1wv", [D, 256], F32, kind="ExternalInput").ap()
    wo1_d = nc.dram_tensor("l1wo", [D, D], F32, kind="ExternalInput").ap()
    wtT_d = nc.dram_tensor("l1wt", [D, D], F32, kind="ExternalInput").ap()
    al1_d = nc.dram_tensor("l1alpha", [1, 1], F32, kind="ExternalInput").ap()
    bi1_d = nc.dram_tensor("l1bias", [1, D], F32, kind="ExternalInput").ap()
    sel_d = nc.dram_tensor("sel8", [NEXP, NEXP * 128], F32, kind="ExternalInput").ap()
    hmask_d = nc.dram_tensor("hmask", [128, 2], BF16, kind="ExternalInput").ap()
    sel2_d = nc.dram_tensor("sel2c", [2, 256], F32, kind="ExternalInput").ap()
    idf32_d = nc.dram_tensor("idf32c", [128, 128], F32, kind="ExternalInput").ap()
    cmask_d = nc.dram_tensor("cmask", [128, 4 * NT], BF16, kind="ExternalInput").ap()
    out_d = nc.dram_tensor("outT", [D, T], F32, kind="ExternalOutput").ap()
    sw1_d = nc.dram_tensor("sw1", [NEXP, 8, 128, 4096], BF16).ap()
    sw2_d = nc.dram_tensor("sw2", [NEXP, 8, 128, 4096], BF16).ap()

    xT_t = xT_d.rearrange("(c p) t -> c p t", p=128)
    out_t = out_d.rearrange("(c p) t -> c p t", p=128)

    with ExitStack() as ctx:
        pb = PB(nc, ctx)
        for s in ("din", "dst", "pe", "dv", "sc", "gp",
                  "w1a", "w1b", "w2a", "w2b", "stga", "stgb", "ppw", "dmisc",
                  "pga", "pgb"):
            pb.sem(s)

        sb = lambda name, shape, dt: ctx.enter_context(nc.sbuf_tensor(name, shape, dt))
        ps = lambda name, shape, dt: ctx.enter_context(nc.psum_tensor(name, shape, dt))

        xT = sb("xTs", [128, DCH * T], F32)
        xnT = sb("xnTs", [128, DCH * T], BF16)
        t1s = sb("t1s", [128, NT], F32)
        onesb = sb("onesb", [1, NT], BF16)
        onesf = sb("onesf", [1, 128], F32)
        invN = sb("invN", [128, 1], BF16)
        onescol = sb("onescol", [128, 1], BF16)
        epscol = sb("epscol", [1, 1], F32)
        colpool = sb("colpool", [128, 32], F32)
        hmask = sb("hmaskS", [128, 2], BF16)
        bankA = ps("bankA", [128, 4 * NT], F32)
        bankB = ps("bankB", [128, 2 * NT], F32)
        bank = [bankA[:, i * NT:(i + 1) * NT] for i in range(4)] + \
               [bankB[:, 0:NT], bankB[:, NT:2 * NT]] + \
               [ps(f"bank{i}", [128, NT], F32) for i in range(6, 8)]

        xch = lambda dc: xT[:, dc * T:(dc + 1) * T]
        xnch = lambda dc: xnT[:, dc * T:(dc + 1) * T]

        for dc in range(DCH):
            pb.dma("sync", xch(dc), xT_t[dc], "din")
        pb.dma("sync", hmask[:], hmask_d[:, :], "din")
        din_inputs = ("din", pb.cnt["din"])
        pb._dep_update((), [f"x{dc}" for dc in range(DCH)] + ["hmask"], din_inputs)

        pb.op("vector", lambda e: e.memset(onesb[:], 1.0), writes=["onesb"])
        pb.op("vector", lambda e: e.memset(onesf[:], 1.0), writes=["onesf"])
        pb.op("vector", lambda e: e.memset(invN[:], float(1.0 / D)), writes=["invN"])
        pb.op("vector", lambda e: e.memset(epscol[:], float(1e-5)), writes=["eps"])
        consts_done = pb.op("vector", lambda e: e.memset(onescol[:], 1.0),
                            writes=["onescol"])

        def emit_layernorm(tag, xT_ready):
            lctx = ExitStack()
            lsb = lambda name, shape, dt: lctx.enter_context(nc.sbuf_tensor(name, shape, dt))
            rows = lsb(f"rows_{tag}", [1, 5 * NT], F32)
            xbf = lsb(f"xbf_{tag}", [128, NT], BF16)
            sqb = lsb(f"sqb_{tag}", [128, NT], BF16)
            rb = lsb(f"rb_{tag}", [128, NT], F32)
            mrb = lsb(f"mrb_{tag}", [128, NT], F32)
            r_mean = rows[0:1, 0 * NT:1 * NT]
            r_ex2 = rows[0:1, 1 * NT:2 * NT]
            r_var = rows[0:1, 2 * NT:3 * NT]
            r_r = rows[0:1, 3 * NT:4 * NT]
            r_mr = rows[0:1, 4 * NT:5 * NT]

            g = tag
            for tau in range(NTAU):
                t0 = tau * NT
                for dc in range(DCH):
                    pb.op("vector", lambda e, dc=dc, t0=t0: e.tensor_copy(
                        xbf[:], xch(dc)[:, t0:t0 + NT]),
                        reads=[f"x{dc}"], writes=[g + "xbf"], wait=list(xT_ready))
                    pb.op("vector", lambda e: e.tensor_tensor(
                        sqb[:], xbf[:], xbf[:], ALU.mult),
                        reads=[g + "xbf"], writes=[g + "sqb"])
                    pb.op("tensor", lambda e, dc=dc: e.matmul(
                        bank[5][0:1, :], invN[:, :], xbf[:],
                        start=(dc == 0), stop=(dc == DCH - 1)),
                        reads=[g + "xbf", "invN"],
                        writes=["b5"] if dc == DCH - 1 else [],
                        wait=(pb._dep_waits((), ["b5"]) if dc == 0 else []))
                    pb.op("tensor", lambda e, dc=dc: e.matmul(
                        bank[6][0:1, :], invN[:, :], sqb[:],
                        start=(dc == 0), stop=(dc == DCH - 1)),
                        reads=[g + "sqb", "invN"],
                        writes=["b6"] if dc == DCH - 1 else [],
                        wait=(pb._dep_waits((), ["b6"]) if dc == 0 else []))
                pb.op("scalar", lambda e: e.copy(r_mean, bank[5][0:1, :]),
                      reads=["b5"], writes=[g + "rm"])
                pb.op("scalar", lambda e: e.copy(r_ex2, bank[6][0:1, :]),
                      reads=["b6"], writes=[g + "re"])
                pb.op("vector", lambda e: e.tensor_tensor(
                    r_var, r_mean, r_mean, ALU.mult),
                    reads=[g + "rm"], writes=[g + "rv"])
                pb.op("vector", lambda e: e.tensor_tensor(
                    r_var, r_ex2, r_var, ALU.subtract),
                    reads=[g + "re", g + "rv"], writes=[g + "rv"])
                pb.op("scalar", lambda e: e.activation(
                    r_var, r_var, AF.Ln, bias=epscol[:]),
                    reads=[g + "rv", "eps"], writes=[g + "rv"])
                pb.op("scalar", lambda e: e.activation(
                    r_r, r_var, AF.Exp, scale=-0.5),
                    reads=[g + "rv"], writes=[g + "rr"])
                pb.op("vector", lambda e: e.tensor_tensor(
                    r_mr, r_mean, r_r, ALU.mult),
                    reads=[g + "rm", g + "rr"], writes=[g + "rmr"])
                pb.op("tensor", lambda e: e.matmul(
                    bank[7][:, :], onesf[:, :], r_r, start=True, stop=True),
                    reads=[g + "rr", "onesf"], writes=["b7"])
                pb.op("vector", lambda e: e.tensor_copy(rb[:], bank[7][:, :]),
                      reads=["b7"], writes=[g + "rb"])
                pb.op("tensor", lambda e: e.matmul(
                    bank[7][:, :], onesf[:, :], r_mr, start=True, stop=True),
                    reads=[g + "rmr", "onesf"], writes=["b7"])
                pb.op("vector", lambda e: e.tensor_copy(mrb[:], bank[7][:, :]),
                      reads=["b7"], writes=[g + "mrb"])
                for dc in range(DCH):
                    pb.op("vector", lambda e, dc=dc, t0=t0: e.tensor_tensor(
                        t1s[:], xch(dc)[:, t0:t0 + NT], rb[:], ALU.mult),
                        reads=[f"x{dc}", g + "rb"], writes=["t1s"])
                    pb.op("vector", lambda e, dc=dc, t0=t0: e.tensor_tensor(
                        xnch(dc)[:, t0:t0 + NT], t1s[:], mrb[:], ALU.subtract),
                        reads=["t1s", g + "mrb"], writes=[f"xn{dc}.{tau}"])
            lctx.close()
            pb.fence()
            return ("dv", pb.cnt["dv"])

        stg = sb("stg", [128, 2 * 2048], F32)
        castb = sb("castb", [128, 2048], BF16)
        wrstg = sb("wrstg", [128, 64], F32)

        def load_cast(dst_bf, src_rows, nchunks, width, tagp, scale=None, dma_q="sync"):
            for i in range(nchunks):
                slot = i % 2
                sem = "stga" if slot == 0 else "stgb"
                pb.dma(dma_q, stg[:, slot * 2048: slot * 2048 + width], src_rows(i),
                       sem, writes=[f"stg{slot}"])
                if scale is None:
                    pb.op("vector", lambda e, i=i, slot=slot: e.tensor_copy(
                        dst_bf[:, i * width:(i + 1) * width],
                        stg[:, slot * 2048: slot * 2048 + width]),
                        reads=[f"stg{slot}"], writes=[f"{tagp}{i}"])
                else:
                    pb.op("vector", lambda e, i=i, slot=slot: e.tensor_scalar(
                        dst_bf[:, i * width:(i + 1) * width],
                        stg[:, slot * 2048: slot * 2048 + width],
                        float(scale), None, ALU.mult),
                        reads=[f"stg{slot}"], writes=[f"{tagp}{i}"])

        def attn_stream_gen(tiles):
            pending_fin = [None, None]

            def flush():
                for h12 in range(2):
                    if pending_fin[h12]:
                        pending_fin[h12]()
                        pending_fin[h12] = None

            for ti, t in enumerate(tiles):
                nkc = t["nkc"]
                for kc in range(nkc + 1):
                    if kc < nkc:
                        pp = 2 * (kc % 2)
                        for h12 in range(2):
                            q4 = pp + h12
                            pb.op("tensor",
                                  lambda e, kc=kc, h12=h12, q4=q4, t=t: e.matmul(
                                      bank[q4][:, :], t["klhsT"](kc, h12),
                                      t["qrhs"][h12], start=True, stop=True),
                                  reads=t["kreads"] + [t["qkey"]], writes=[f"b{q4}"])
                    if kc == 1:
                        flush()
                    if kc >= 1:
                        kp = kc - 1
                        qp = 2 * (kp % 2)
                        for h12 in range(2):
                            q4 = qp + h12
                            pb.op("tensor",
                                  lambda e, kp=kp, h12=h12, q4=q4, t=t, nkc=nkc: e.matmul(
                                      bank[4 + h12][0:65, :], t["vap"](kp, h12),
                                      t["expb"][:, q4 * NT:(q4 + 1) * NT],
                                      start=(kp == 0), stop=(kp == nkc - 1)),
                                  reads=[f"e{q4}", t["vkeys"][h12]],
                                  writes=([f"b{4 + h12}"] if kp == nkc - 1 else []),
                                  wait=(pb._dep_waits((), [f"b{4 + h12}"]) if kp == 0 else []))
                    if kc < nkc:
                        pp = 2 * (kc % 2)
                        pb.op("scalar",
                              lambda e, pp=pp, t=t: e.activation(
                                  t["expb"][:, pp * NT:(pp + 2) * NT],
                                  bankA[:, pp * NT:(pp + 2) * NT], AF.Exp,
                                  bias=t["negcol"]),
                              reads=[f"b{pp}", f"b{pp + 1}", t["ckey"]],
                              writes=[f"e{pp}", f"e{pp + 1}"])
                        if t.get("mask_from") is not None and kc >= t["mask_from"]:
                            o = kc - t["mask_from"]
                            for h12 in range(2):
                                q4 = pp + h12
                                pb.op("vector",
                                      lambda e, q4=q4, o=o, t=t: e.tensor_tensor(
                                          t["expb"][:, q4 * NT:(q4 + 1) * NT],
                                          t["expb"][:, q4 * NT:(q4 + 1) * NT],
                                          t["cmask"][:, o * NT:(o + 1) * NT], ALU.mult),
                                      reads=[f"e{q4}", "cmask"], writes=[f"e{q4}"])
                    yield
                for h12 in range(2):
                    pending_fin[h12] = (lambda t=t, h12=h12: t["fins"][h12]())
            flush()

        def attn_stream(tiles):
            for _ in attn_stream_gen(tiles):
                pass

        def emit_attn0(xn_done):
            lctx = ExitStack()
            lsb = lambda name, shape, dt: lctx.enter_context(nc.sbuf_tensor(name, shape, dt))
            wqbf = lsb("wqbf", [128, DCH * 1024], BF16)
            wkbf = lsb("wkbf", [128, DCH * 1024], BF16)
            bqbf = lsb("bqbf", [1, D], BF16)
            bkbf = lsb("bkbf", [1, D], BF16)
            qTb = [lsb("qT0a", [128, T], BF16), lsb("qT0b", [128, T], BF16)]
            kTb = [lsb("kT0a", [128, T], BF16), lsb("kT0b", [128, T], BF16)]
            vAb = [lsb("vA0a", [128, 16 * 65], BF16), lsb("vA0b", [128, 16 * 65], BF16)]
            vBb = [lsb("vB0a", [128, 16 * 65], BF16), lsb("vB0b", [128, 16 * 65], BF16)]
            expb = lsb("exp0", [128, 4 * NT], BF16)
            rrow = lsb("rrow0", [1, 2 * NT], F32)
            rhl = lsb("rhl0", [1, 4 * NT], BF16)
            rb2 = lsb("rb20", [128, 2 * NT], F32)

            wq_r = wq0_d.rearrange("(c p) n -> c p n", p=128)
            wk_r = wk0_d.rearrange("(c p) n -> c p n", p=128)
            load_cast(wqbf, lambda i: wq_r[i], DCH, 1024, "wq", scale=1.0 / 32.0)
            load_cast(wkbf, lambda i: wk_r[i], DCH, 1024, "wk")
            pb.dma("sync", rrow[0:1, 0:NT], bq0_d[0:1, 0:NT], "dmisc")
            pb.dma("sync", rrow[0:1, NT:2 * NT], bq0_d[0:1, NT:2 * NT], "dmisc")
            bq_half = ("dmisc", pb.cnt["dmisc"])
            pb._dep_update((), ["bqstg"], bq_half)
            pb.op("vector", lambda e: e.tensor_scalar(
                bqbf[0:1, 0:NT], rrow[0:1, 0:NT], 1.0 / 32.0, None, ALU.mult),
                reads=["bqstg"], writes=["bqbf"])
            pb.op("vector", lambda e: e.tensor_scalar(
                bqbf[0:1, NT:2 * NT], rrow[0:1, NT:2 * NT], 1.0 / 32.0, None, ALU.mult),
                reads=["bqstg"], writes=["bqbf"])
            pb.dma("sync", rrow[0:1, 0:NT], bk0_d[0:1, 0:NT], "dmisc",
                   reads=["bqbf"])
            pb.dma("sync", rrow[0:1, NT:2 * NT], bk0_d[0:1, NT:2 * NT], "dmisc",
                   reads=["bqbf"])
            bk_half = ("dmisc", pb.cnt["dmisc"])
            pb._dep_update((), ["bkstg"], bk_half)
            pb.op("vector", lambda e: e.tensor_copy(bkbf[0:1, 0:NT], rrow[0:1, 0:NT]),
                  reads=["bkstg"], writes=["bkbf"])
            pb.op("vector", lambda e: e.tensor_copy(bkbf[0:1, NT:2 * NT], rrow[0:1, NT:2 * NT]),
                  reads=["bkstg"], writes=["bkbf", "rrow0", "rrow1"])
            for par in range(2):
                for vbuf, key in ((vAb[par], f"vA{par}"), (vBb[par], f"vB{par}")):
                    for j in range(16):
                        pb.op("vector", lambda e, vbuf=vbuf, j=j: e.memset(
                            vbuf[:, j * 65 + 64: j * 65 + 65], 1.0),
                            writes=[key + "ones"])

            xnr = [xn_done]
            CP = colpool

            def mk_fin(h12, tgt_fn):
                def fin():
                    avb = 4 + h12
                    bb = h12
                    hi = rhl[0:1, (2 * h12) * NT:(2 * h12 + 1) * NT]
                    lo = rhl[0:1, (2 * h12 + 1) * NT:(2 * h12 + 2) * NT]
                    rr = rrow[0:1, h12 * NT:(h12 + 1) * NT]
                    pb.op("scalar", lambda e: e.activation(
                        rr, bank[avb][64:65, :], AF.Ln),
                        reads=[f"b{avb}"], writes=[f"rrow{h12}"])
                    pb.op("scalar", lambda e: e.activation(
                        rr, rr, AF.Exp, scale=-1.0),
                        reads=[f"rrow{h12}"], writes=[f"rrow{h12}"])
                    pb.op("vector", lambda e: e.tensor_copy(hi, rr),
                          reads=[f"rrow{h12}"], writes=[f"rhi{h12}"])
                    pb.op("vector", lambda e: e.tensor_tensor(lo, rr, hi, ALU.subtract),
                          reads=[f"rrow{h12}", f"rhi{h12}"], writes=[f"rlo{h12}"])
                    pb.op("tensor", lambda e: e.matmul(
                        bank[bb][:, :], onesb[0:1, 0:128], hi, start=True, stop=False),
                        reads=[f"rhi{h12}", "onesb"],
                        wait=pb._dep_waits((), [f"b{bb}"]))
                    pb.op("tensor", lambda e: e.matmul(
                        bank[bb][:, :], onesb[0:1, 0:128], lo, start=False, stop=True),
                        reads=[f"rlo{h12}", "onesb"], writes=[f"b{bb}"])
                    rbs = rb2[:, h12 * NT:(h12 + 1) * NT]
                    pb.op("vector", lambda e: e.tensor_copy(rbs, bank[bb][:, :]),
                          reads=[f"b{bb}"], writes=[f"rb{h12}"])
                    tgt_fn(avb, rbs, h12)
                return fin

            def l0_tgt(hp, hbase, t0):
                def tgt(avb, rbs, h12):
                    pb.op("vector", lambda e, avb=avb, rbs=rbs: e.tensor_tensor(
                        t1s[hbase:hbase + 64, :], bank[avb][0:64, :],
                        rbs[0:64, :], ALU.mult),
                        reads=[f"b{avb}", f"rb{h12}"], writes=["t1s"])
                    pb.op("vector", lambda e: e.tensor_tensor(
                        xch(hp)[hbase:hbase + 64, t0:t0 + NT],
                        xch(hp)[hbase:hbase + 64, t0:t0 + NT],
                        t1s[hbase:hbase + 64, :], ALU.add),
                        reads=["t1s", f"x{hp}"], writes=[f"x{hp}"])
                return tgt

            def prologue_chunks(hp):
                par = hp % 2
                qT, kT, vA, vB = qTb[par], kTb[par], vAb[par], vBb[par]
                ch = []
                for (dst, wbf, bbf, wkey, bkey, dkey) in (
                        (qT, wqbf, bqbf, "wq", "bqbf", f"qT{par}"),
                        (kT, wkbf, bkbf, "wk", "bkbf", f"kT{par}")):
                    for tau in range(NTAU):
                        def proj(tau=tau, dst=dst, wbf=wbf, bbf=bbf,
                                 wkey=wkey, bkey=bkey, dkey=dkey, hp=hp):
                            t0 = tau * NT
                            bnk = bank[6 + tau % 2]
                            bkkey = f"b{6 + tau % 2}"
                            for dc in range(DCH):
                                pb.op("tensor", lambda e, dc=dc, t0=t0, wbf=wbf, hp=hp, bnk=bnk:
                                      e.matmul(bnk[:, :],
                                               wbf[:, dc * 1024 + hp * 128: dc * 1024 + (hp + 1) * 128],
                                               xnch(dc)[:, t0:t0 + NT],
                                               start=(dc == 0), stop=False),
                                      reads=[f"{wkey}{dc}", f"xn{dc}.{tau}"],
                                      wait=(pb._dep_waits((), [bkkey]) if dc == 0 else []) + xnr)
                            pb.op("tensor", lambda e, t0=t0, bbf=bbf, hp=hp, bnk=bnk:
                                  e.matmul(bnk[:, :],
                                           bbf[0:1, hp * 128:(hp + 1) * 128], onesb[0:1, :],
                                           start=False, stop=True),
                                  reads=[bkey, "onesb"], writes=[bkkey])
                            pb.op("scalar", lambda e, t0=t0, dst=dst, bnk=bnk: e.copy(
                                dst[:, t0:t0 + NT], bnk[:, :]),
                                reads=[bkkey], writes=[dkey])
                        ch.append(proj)
                for mt in range(16):
                    def ktm(mt=mt, hp=hp, par=par, vA=vA, vB=vB):
                        bnk = bank[6 + (mt % 2)]
                        bkkey = f"b{6 + (mt % 2)}"
                        for dc in range(DCH):
                            pb.op("tensor", lambda e, dc=dc, mt=mt, hp=hp, bnk=bnk:
                                  e.matmul(bnk[:, 0:128],
                                           xnch(dc)[:, mt * 128:(mt + 1) * 128],
                                           wkbf[:, dc * 1024 + hp * 128: dc * 1024 + (hp + 1) * 128],
                                           start=(dc == 0), stop=False),
                                  reads=[f"wk{dc}"] + [f"xn{dc}.{mt // 4}"],
                                  wait=(pb._dep_waits((), [bkkey]) if dc == 0 else []) + xnr)
                        pb.op("tensor", lambda e, mt=mt, hp=hp, bnk=bnk:
                              e.matmul(bnk[:, 0:128], onesb[0:1, 0:128],
                                       bkbf[0:1, hp * 128:(hp + 1) * 128],
                                       start=False, stop=True),
                              reads=["bkbf", "onesb"], writes=[bkkey])
                        pb.op("scalar", lambda e, mt=mt, bnk=bnk, vA=vA: e.copy(
                            vA[:, mt * 65: mt * 65 + 64], bnk[:, 0:64]),
                            reads=[bkkey, f"vA{par}ones"], writes=[f"vA{par}"])
                        pb.op("scalar", lambda e, mt=mt, bnk=bnk, vB=vB: e.copy(
                            vB[:, mt * 65: mt * 65 + 64], bnk[:, 64:128]),
                            reads=[bkkey, f"vB{par}ones"], writes=[f"vB{par}"])
                    ch.append(ktm)

                def norms(hp=hp, par=par, qT=qT, kT=kT):
                    for (src, skey, coli) in ((qT, f"qT{par}", 0), (kT, f"kT{par}", 4)):
                        pb.op("vector", lambda e, src=src: e.tensor_tensor(
                            expb[:, :], src[:], src[:], ALU.mult),
                            reads=[skey], writes=["e0", "e1", "e2", "e3"])
                        for tau in range(NTAU):
                            pb.op("tensor", lambda e, tau=tau: e.matmul(
                                bank[7][0:1, :], onescol[:, :],
                                expb[:, tau * NT:(tau + 1) * NT], start=True, stop=True),
                                reads=[f"e{tau}", "onescol"], writes=["b7"])
                            pb.op("vector", lambda e, tau=tau, coli=coli: e.reduce_max(
                                CP[0:1, coli + tau:coli + tau + 1], bank[7][0:1, :], AX.X),
                                reads=["b7"], writes=[f"cp{coli + tau}"])
                    pb.op("vector", lambda e: e.reduce_max(CP[0:1, 8:9], CP[0:1, 0:4], AX.X),
                          reads=[f"cp{i}" for i in range(4)], writes=["cpq"])
                    pb.op("vector", lambda e: e.reduce_max(CP[0:1, 9:10], CP[0:1, 4:8], AX.X),
                          reads=[f"cp{i}" for i in range(4, 8)], writes=["cpk"])
                    pb.op("vector", lambda e: e.tensor_tensor(
                        CP[0:1, 10:11], CP[0:1, 8:9], CP[0:1, 9:10], ALU.mult),
                        reads=["cpq", "cpk"], writes=["cpm"])
                    pb.op("scalar", lambda e: e.activation(
                        CP[0:1, 10:11], CP[0:1, 10:11], AF.Ln),
                        reads=["cpm"], writes=["cpm"])
                    pb.op("scalar", lambda e: e.activation(
                        CP[0:1, 10:11], CP[0:1, 10:11], AF.Exp, scale=0.5),
                        reads=["cpm"], writes=["cpm"])
                    pb.op("vector", lambda e: e.tensor_scalar(
                        CP[0:1, 11:12], CP[0:1, 10:11], -1.0, None, ALU.mult),
                        reads=["cpm"], writes=["cpn"])
                    pb.op("tensor", lambda e: e.matmul(
                        bank[7][:, 0:1], onesf[:, :], CP[0:1, 11:12],
                        start=True, stop=True),
                        reads=["cpn", "onesf"], writes=["b7"])
                    pb.op("vector", lambda e, par=par: e.tensor_copy(
                        CP[:, 12 + par:13 + par], bank[7][:, 0:1]),
                        reads=["b7"], writes=[f"neg{par}"])
                ch.append(norms)
                return ch

            def build_tiles(hp):
                par = hp % 2
                qT, kT, vA, vB = qTb[par], kTb[par], vAb[par], vBb[par]
                tiles = []
                for tau in range(NTAU):
                    t0 = tau * NT
                    tiles.append(dict(
                        nkc=16,
                        klhsT=(lambda kc, h12, kT=kT: kT[64 * h12:64 * h12 + 64, kc * 128:(kc + 1) * 128]),
                        qrhs=[qT[0:64, t0:t0 + NT], qT[64:128, t0:t0 + NT]],
                        kreads=[f"kT{par}"], qkey=f"qT{par}",
                        vap=(lambda kc, h12, vA=vA, vB=vB: (vA if h12 == 0 else vB)[:, kc * 65:(kc + 1) * 65]),
                        vkeys=[f"vA{par}", f"vB{par}"], expb=expb,
                        negcol=CP[:, 12 + par:13 + par], ckey=f"neg{par}",
                        fins=[mk_fin(0, l0_tgt(hp, 0, t0)),
                              mk_fin(1, l0_tgt(hp, 64, t0))],
                    ))
                return tiles

            gen = None
            for hp in range(DCH):
                for chf in prologue_chunks(hp):
                    chf()
                    if gen is not None:
                        for _ in range(5):
                            next(gen, None)
                if gen is not None:
                    for _ in gen:
                        pass
                gen = attn_stream_gen(build_tiles(hp))
            for _ in gen:
                pass
            lctx.close()
            pb.fence(exempt=("stg0", "stg1", "castb0"))

        def emit_moe0(xn_done):
            lctx = ExitStack()
            lsb = lambda name, shape, dt: lctx.enter_context(nc.sbuf_tensor(name, shape, dt))
            wrbf = lsb("wrbf", [128, DCH * NEXP], BF16)
            gtm = lsb("gtm", [128, 128], F32)
            ctm = lsb("ctm", [128, 128], F32)
            cT = lsb("cT", [NEXP, T], F32)
            selb = lsb("selb", [NEXP, NEXP * 128], F32)
            idf32 = lsb("idf32", [128, 128], F32)
            hid = lsb("hid", [128, NFC * NT], BF16)
            w1buf = lsb("w1buf", [128, 2 * 4096], BF16)
            w2buf = lsb("w2buf", [128, 2 * 4096], BF16)
            cbs = lsb("cbs", [128, NT], F32)
            mtmp = lsb("mtmp", [128, 2 * NT], F32)
            CP = colpool
            xnr = [xn_done]

            pb.dma("sync", selb[:, :], sel_d[:, :], "dmisc", writes=["selb"])
            pb.dma("sync", idf32[:, :], idf32_d[:, :], "dmisc", writes=["idf32"])
            pb.dma("sync", wrstg[:, :].rearrange("p (c n) -> p c n", n=NEXP),
                   wr0_d.rearrange("(c p) n -> p c n", p=128), "dmisc",
                   writes=["wrstg"])
            pb.op("vector", lambda e: e.tensor_copy(wrbf[:, :], wrstg[:, :]),
                  reads=["wrstg"], writes=["wrbf"])

            for mt in range(16):
                bnk = bank[2 + mt % 2]
                bkkey = f"b{2 + mt % 2}"
                for dc in range(DCH):
                    pb.op("tensor", lambda e, dc=dc, mt=mt, bnk=bnk: e.matmul(
                        bnk[:, 0:NEXP],
                        xnch(dc)[:, mt * 128:(mt + 1) * 128],
                        wrbf[:, dc * NEXP:(dc + 1) * NEXP],
                        start=(dc == 0), stop=(dc == DCH - 1)),
                        reads=["wrbf", f"xn{dc}.{mt // 4}"],
                        writes=[bkkey] if dc == DCH - 1 else [],
                        wait=(pb._dep_waits((), [bkkey]) if dc == 0 else []) + xnr)
                pb.op("vector", lambda e, bnk=bnk: e.tensor_copy(gtm[:, 0:NEXP], bnk[:, 0:NEXP]),
                      reads=[bkkey], writes=["gtm"])
                pb.op("vector", lambda e: e.reduce_max(CP[:, 12:13], gtm[:, 0:NEXP], AX.X),
                      reads=["gtm"], writes=["c12"])
                pb.op("vector", lambda e: e.tensor_scalar(
                    ctm[:, 0:NEXP], gtm[:, 0:NEXP], CP[:, 12:13], None, ALU.is_ge),
                    reads=["gtm", "c12"], writes=["mask1"])
                pb.op("vector", lambda e: e.scalar_tensor_tensor(
                    gtm[:, 8:16], ctm[:, 0:NEXP], -1e30, gtm[:, 0:NEXP],
                    ALU.mult, ALU.add),
                    reads=["gtm", "mask1"], writes=["gwo"])
                pb.op("vector", lambda e: e.reduce_max(CP[:, 13:14], gtm[:, 8:16], AX.X),
                      reads=["gwo"], writes=["c13"])
                pb.op("vector", lambda e: e.tensor_scalar(
                    ctm[:, 8:16], gtm[:, 8:16], CP[:, 13:14], None, ALU.is_ge),
                    reads=["gwo", "c13"], writes=["mask2"])
                pb.op("vector", lambda e: e.tensor_tensor(
                    CP[:, 14:15], CP[:, 13:14], CP[:, 12:13], ALU.subtract),
                    reads=["c12", "c13"], writes=["c14"])
                pb.op("scalar", lambda e: e.activation(
                    CP[:, 14:15], CP[:, 14:15], AF.Exp),
                    reads=["c14"], writes=["c14"])
                pb.op("vector", lambda e: e.tensor_scalar(
                    CP[:, 15:16], CP[:, 14:15], 1.0, None, ALU.add),
                    reads=["c14"], writes=["c15"])
                pb.op("vector", lambda e: e.reciprocal(CP[:, 15:16], CP[:, 15:16]),
                      reads=["c15"], writes=["c15"])
                pb.op("vector", lambda e: e.tensor_tensor(
                    CP[:, 14:15], CP[:, 14:15], CP[:, 15:16], ALU.mult),
                    reads=["c14", "c15"], writes=["c14"])
                pb.op("vector", lambda e: e.tensor_scalar(
                    ctm[:, 0:NEXP], ctm[:, 0:NEXP], CP[:, 15:16], None, ALU.mult),
                    reads=["mask1", "c15"], writes=["mask1"])
                pb.op("vector", lambda e: e.scalar_tensor_tensor(
                    ctm[:, 0:NEXP], ctm[:, 8:16], CP[:, 14:15], ctm[:, 0:NEXP],
                    ALU.mult, ALU.add),
                    reads=["mask1", "mask2", "c14"], writes=["ctile"])
                pb.op("tensor", lambda e: e.transpose(
                    bank[4][0:NEXP, 0:128], ctm[:, 0:NEXP], idf32[:, :]),
                    reads=["ctile", "idf32"], writes=["b4"])
                pb.op("vector", lambda e, mt=mt: e.tensor_copy(
                    cT[0:NEXP, mt * 128:(mt + 1) * 128], bank[4][0:NEXP, 0:128]),
                    reads=["b4"], writes=["cT"])

            for ex in range(NEXP):
                for fg in range(8):
                    for half in range(2):
                        slot = (fg * 2 + half) % 2
                        pb.dma("gpsimd",
                               stg[:, slot * 2048:(slot + 1) * 2048].rearrange(
                                   "p (c f) -> p c f", f=NT),
                               w1_d[ex].rearrange("(c p) (g f) -> g p c f",
                                                  p=128, f=NT)[fg][:, half * 4:(half + 1) * 4, :],
                               "pga" if slot == 0 else "pgb",
                               writes=[f"stg{slot}"])
                        pb.op("gpsimd", lambda e, slot=slot: e.tensor_copy(
                            castb[:, 0:2048],
                            stg[:, slot * 2048:(slot + 1) * 2048]),
                            reads=[f"stg{slot}"], writes=["castb0"])
                        pb.dma("gpsimd", sw1_d[ex, fg, :, half * 2048:(half + 1) * 2048],
                               castb[:, 0:2048],
                               "ppw", reads=["castb0"], serialize=True)
                for dc in range(8):
                    for qt in range(4):
                        slot = qt % 2
                        pb.dma("gpsimd",
                               stg[:, slot * 2048 + 0:slot * 2048 + 1024].rearrange(
                                   "p (fc j) -> p fc j", j=128),
                               w2_d[ex].rearrange("(fc p) (c j) -> c p fc j",
                                                  p=128, j=128)[dc][:, qt * 8:(qt + 1) * 8, :],
                               "pga" if slot == 0 else "pgb",
                               writes=[f"stg{slot}"])
                        pb.op("gpsimd", lambda e, slot=slot: e.tensor_copy(
                            castb[:, 0:1024],
                            stg[:, slot * 2048:slot * 2048 + 1024]),
                            reads=[f"stg{slot}"], writes=["castb0"])
                        pb.dma("gpsimd", sw2_d[ex, dc, :, qt * 1024:(qt + 1) * 1024],
                               castb[:, 0:1024],
                               "ppw", reads=["castb0"], serialize=True)
                pb._dep_update((), [f"pp{ex}"], ("ppw", pb.cnt["ppw"]))

            for ex in range(NEXP):
                for tau in range(NTAU):
                    t0 = tau * NT
                    pb.op("tensor", lambda e, ex=ex, t0=t0: e.matmul(
                        bank[5][:, :], selb[0:NEXP, ex * 128:(ex + 1) * 128],
                        cT[0:NEXP, t0:t0 + NT], start=True, stop=True),
                        reads=["cT", "selb"], writes=["b5"])
                    pb.op("vector", lambda e: e.tensor_copy(cbs[:], bank[5][:, :]),
                          reads=["b5"], writes=["cbs"])
                    for fg in range(8):
                        slot = fg % 2
                        wsem = "w1a" if slot == 0 else "w1b"
                        pb.dma("sync", w1buf[:, slot * 4096:(slot + 1) * 4096],
                               sw1_d[ex, fg], wsem,
                               reads=[f"pp{ex}"], writes=[f"w1s{slot}"])
                        for fp in range(4):
                            f = fg * 4 + fp
                            rot = (fg * 4 + fp) % 4
                            bnk = bank[rot]
                            bkkey = f"b{rot}"
                            for dc in range(DCH):
                                pb.op("tensor", lambda e, dc=dc, fp=fp, slot=slot, t0=t0, bnk=bnk:
                                      e.matmul(bnk[:, :],
                                               w1buf[:, slot * 4096 + dc * NT + fp * 128:
                                                      slot * 4096 + dc * NT + (fp + 1) * 128],
                                               xnch(dc)[:, t0:t0 + NT],
                                               start=(dc == 0), stop=(dc == DCH - 1)),
                                      reads=[f"w1s{slot}", f"xn{dc}.{tau}"],
                                      writes=[bkkey] if dc == DCH - 1 else [],
                                      wait=(pb._dep_waits((), [bkkey]) if dc == 0 else []) + xnr)
                            pb.op("scalar", lambda e, f=f, bnk=bnk: e.activation(
                                hid[:, f * NT:(f + 1) * NT], bnk[:, :], AF.Relu),
                                reads=[bkkey], writes=[f"hid{f % 4}"])
                    for dc in range(DCH):
                        slot = dc % 2
                        wsem = "w2a" if slot == 0 else "w2b"
                        pb.dma("sync", w2buf[:, slot * 4096:(slot + 1) * 4096],
                               sw2_d[ex, dc], wsem,
                               reads=[f"pp{ex}"], writes=[f"w2s{slot}"])
                        bnk = bank[dc % 4]
                        bkkey = f"b{dc % 4}"
                        for fc in range(NFC):
                            pb.op("tensor", lambda e, fc=fc, slot=slot, bnk=bnk:
                                  e.matmul(bnk[:, :],
                                           w2buf[:, slot * 4096 + fc * 128:slot * 4096 + (fc + 1) * 128],
                                           hid[:, fc * NT:(fc + 1) * NT],
                                           start=(fc == 0), stop=(fc == NFC - 1)),
                                  reads=[f"w2s{slot}"] + [f"hid{j}" for j in range(4)],
                                  writes=[bkkey] if fc == NFC - 1 else [],
                                  wait=(pb._dep_waits((), [bkkey]) if fc == 0 else []))
                        mslot = dc % 2
                        pb.op("vector", lambda e, bnk=bnk, mslot=mslot: e.tensor_tensor(
                            mtmp[:, mslot * NT:(mslot + 1) * NT], bnk[:, :],
                            cbs[:], ALU.mult),
                            reads=[bkkey, "cbs"], writes=[f"mtmp{mslot}"])
                        pb.op("vector", lambda e, dc=dc, t0=t0, mslot=mslot: e.tensor_tensor(
                            xch(dc)[:, t0:t0 + NT], xch(dc)[:, t0:t0 + NT],
                            mtmp[:, mslot * NT:(mslot + 1) * NT], ALU.add),
                            reads=[f"mtmp{mslot}", f"x{dc}"], writes=[f"x{dc}"])
            lctx.close()
            pb.fence()

        def emit_attn1(xn_done):
            yctx = ExitStack()
            yT = yctx.enter_context(nc.sbuf_tensor("yT1", [128, DCH * T], BF16))
            ych = lambda c: yT[:, c * T:(c + 1) * T]
            lctx = ExitStack()
            lsb = lambda name, shape, dt: lctx.enter_context(nc.sbuf_tensor(name, shape, dt))
            TA = lsb("TA1", [128, T], BF16)
            TB = lsb("TB1", [128, T], BF16)
            invfr = lsb("invfr1", [1, 128], F32)
            c11 = lsb("c11", [1, 1], F32)
            sgncol = lsb("sgncol", [128, 1], F32)
            kT1 = lsb("kT1", [128, 2 * T], BF16)
            qT1 = lsb("qT1", [128, T], BF16)
            vG = lsb("vG1", [128, 4 * 16 * 65], BF16)
            rtmpA = lsb("rtmpA", [128, NT], BF16)
            rtmpB = lsb("rtmpB", [128, NT], BF16)
            expb = lsb("exp1", [128, 4 * NT], BF16)
            rb2 = lsb("rb21", [128, 2 * NT], F32)
            CP = colpool
            xnr = [xn_done]
            actx = ExitStack()
            asb = lambda name, shape, dt: actx.enter_context(nc.sbuf_tensor(name, shape, dt))
            rsc = asb("rsc1", [128, 3 * NT], F32)
            rni = asb("rni1", [128, NT], I32)
            iota_t = asb("iota1", [1, NT], F32)



            pb.op("gpsimd", lambda e: e.iota(
                invfr[:], [[0, 4], [1, 32]], channel_multiplier=0,
                allow_small_or_imprecise_dtypes=True), writes=["invfr"])
            pb.op("vector", lambda e: e.memset(c11[:], 10000.0), writes=["c11"])
            pb.op("scalar", lambda e: e.activation(c11[:], c11[:], AF.Ln),
                  reads=["c11"], writes=["c11"])
            pb.op("vector", lambda e: e.tensor_scalar(
                invfr[:], invfr[:], float(1.0 / 32.0), None, ALU.mult),
                reads=["invfr"], writes=["invfr"])
            pb.op("vector", lambda e: e.tensor_scalar(
                invfr[:], invfr[:], c11[0:1, 0:1], None, ALU.mult),
                reads=["invfr", "c11"], writes=["invfr"])
            pb.op("scalar", lambda e: e.activation(
                invfr[:], invfr[:], AF.Exp, scale=-1.0),
                reads=["invfr"], writes=["invfr"])
            pb.op("vector", lambda e: e.memset(sgncol[0:32, :], -1.0), writes=["sgn"])
            pb.op("vector", lambda e: e.memset(sgncol[32:64, :], 1.0), writes=["sgn"])
            pb.op("vector", lambda e: e.memset(sgncol[64:96, :], -1.0), writes=["sgn"])
            pb.op("vector", lambda e: e.memset(sgncol[96:128, :], 1.0), writes=["sgn"])

            I2P_HI = float(np.float32(1.0 / (2 * np.pi)))
            I2P_LO = float(np.float64(1.0 / (2 * np.pi)) - np.float64(np.float32(1.0 / (2 * np.pi))))
            TWO_PI_HI = float(np.float32(2 * np.pi))
            TWO_PI_LO = float(np.float64(2 * np.pi) - np.float64(np.float32(2 * np.pi)))
            PC = float(np.float32(3.1415925))
            r_ang = rsc[:, 0:NT]
            r_y = rsc[:, NT:2 * NT]
            r_nf = rsc[:, 2 * NT:3 * NT]

            def reduce_sin(dst, dkey, pre_add, signed):
                if pre_add != 0.0:
                    pb.op("vector", lambda e: e.tensor_scalar(
                        r_ang[:, :], r_ang[:, :], pre_add, None, ALU.add),
                        reads=["r_ang"], writes=["r_ang"])
                pb.op("vector", lambda e: e.tensor_scalar(
                    r_y[:, :], r_ang[:, :], I2P_LO, None, ALU.mult),
                    reads=["r_ang"], writes=["r_y"])
                pb.op("vector", lambda e: e.scalar_tensor_tensor(
                    r_y[:, :], r_ang[:, :], I2P_HI, r_y[:, :], ALU.mult, ALU.add),
                    reads=["r_ang", "r_y"], writes=["r_y"])
                pb.op("vector", lambda e: e.tensor_copy(rni[:, :], r_y[:, :]),
                      reads=["r_y"], writes=["rni"])
                pb.op("vector", lambda e: e.tensor_copy(r_nf[:, :], rni[:, :]),
                      reads=["rni"], writes=["r_nf"])
                pb.op("vector", lambda e: e.tensor_tensor(
                    r_y[:, :], r_y[:, :], r_nf[:, :], ALU.subtract),
                    reads=["r_y", "r_nf"], writes=["r_y"])
                pb.op("vector", lambda e: e.tensor_scalar(
                    r_y[:, :], r_y[:, :], 0.5, None, ALU.is_ge),
                    reads=["r_y"], writes=["r_y"])
                pb.op("vector", lambda e: e.tensor_tensor(
                    r_nf[:, :], r_nf[:, :], r_y[:, :], ALU.add),
                    reads=["r_nf", "r_y"], writes=["r_nf"])
                pb.op("vector", lambda e: e.scalar_tensor_tensor(
                    r_ang[:, :], r_nf[:, :], -TWO_PI_HI, r_ang[:, :], ALU.mult, ALU.add),
                    reads=["r_nf", "r_ang"], writes=["r_ang"])
                pb.op("vector", lambda e: e.scalar_tensor_tensor(
                    r_ang[:, :], r_nf[:, :], -TWO_PI_LO, r_ang[:, :], ALU.mult, ALU.add),
                    reads=["r_nf", "r_ang"], writes=["r_ang"])
                pb.op("vector", lambda e: e.tensor_scalar(
                    r_ang[:, :], r_ang[:, :], -PC, PC, ALU.max, ALU.min),
                    reads=["r_ang"], writes=["r_ang"])
                if signed:
                    pb.op("scalar", lambda e: e.activation(
                        r_y[:, :], r_ang[:, :], AF.Sin),
                        reads=["r_ang"], writes=["r_y"])
                    pb.op("vector", lambda e, dst=dst: e.tensor_scalar(
                        dst, r_y[:, :], sgncol[:, :], None, ALU.mult),
                        reads=["r_y", "sgn"], writes=[dkey])
                else:
                    pb.op("scalar", lambda e, dst=dst: e.activation(
                        dst, r_ang[:, :], AF.Sin),
                        reads=["r_ang"], writes=[dkey])

            for tau in range(NTAU):
                t0 = tau * NT
                pb.op("gpsimd", lambda e, t0=t0: e.iota(
                    iota_t[:], [[1, NT]], base=t0, channel_multiplier=0,
                    allow_small_or_imprecise_dtypes=True), writes=["iota_t"])
                pb.op("tensor", lambda e, t0=t0: e.matmul(
                    bank[7][:, :], invfr[0:1, :], iota_t[0:1, 0:NT],
                    start=True, stop=True),
                    reads=["invfr", "iota_t"], writes=["b7"])
                pb.op("vector", lambda e: e.tensor_copy(r_ang[:, :], bank[7][:, :]),
                      reads=["b7"], writes=["r_ang", "angsave"])
                reduce_sin(TA[:, t0:t0 + NT], "TA", float(np.pi / 2), False)
                pb.op("vector", lambda e: e.tensor_copy(r_ang[:, :], bank[7][:, :]),
                      reads=["b7", "angsave"], writes=["r_ang"])
                reduce_sin(TB[:, t0:t0 + NT], "TB", 0.0, True)

            actx.close()
            pb.fence()

            bctx = ExitStack()
            wkvbf = bctx.enter_context(nc.sbuf_tensor("wkvbf", [128, 2 * DCH * 256], BF16))
            wk_r = wk1_d.rearrange("(c p) n -> c p n", p=128)
            wv_r = wv1_d.rearrange("(c p) n -> c p n", p=128)
            load_cast(wkvbf, lambda i: wk_r[i], DCH, 256, "wk1")
            load_cast(wkvbf[:, DCH * 256:], lambda i: wv_r[i], DCH, 256, "wv1")

            def rope_evac(bnk, bkkey, dst, dcol0, dkey, t0):
                pb.op("scalar", lambda e, bnk=bnk: e.copy(rtmpA[:, :], bnk[:, :]),
                      reads=[bkkey], writes=["rtA"])
                for gidx, (d0, s0) in enumerate(((0, 32), (32, 0), (64, 96), (96, 64))):
                    pb.op("scalar", lambda e, bnk=bnk, d0=d0, s0=s0: e.copy(
                        rtmpB[d0:d0 + 32, :], bnk[s0:s0 + 32, :]),
                        reads=[bkkey], writes=["rtB"] if gidx == 3 else [],
                        wait=(pb._dep_waits((), ["rtB"]) if gidx == 0 else []))
                pb.op("vector", lambda e, t0=t0: e.tensor_tensor(
                    rtmpA[:, :], rtmpA[:, :], TA[:, t0:t0 + NT], ALU.mult),
                    reads=["rtA", "TA"], writes=["rtA"])
                pb.op("vector", lambda e, t0=t0: e.tensor_tensor(
                    rtmpB[:, :], rtmpB[:, :], TB[:, t0:t0 + NT], ALU.mult),
                    reads=["rtB", "TB"], writes=["rtB"])
                pb.op("vector", lambda e, dst=dst, t0=t0, dcol0=dcol0: e.tensor_tensor(
                    dst[:, dcol0 + t0:dcol0 + t0 + NT], rtmpA[:, :], rtmpB[:, :], ALU.add),
                    reads=["rtA", "rtB"], writes=[dkey])

            for g in range(4):
                for j in range(16):
                    pb.op("vector", lambda e, g=g, j=j: e.memset(
                        vG[:, g * 1040 + j * 65 + 64: g * 1040 + j * 65 + 65], 1.0),
                        writes=["vGones"])
            for kvc in range(2):
                for tau in range(NTAU):
                    t0 = tau * NT
                    pj = 6 + tau % 2
                    for dc in range(DCH):
                        pb.op("tensor", lambda e, dc=dc, kvc=kvc, t0=t0, pj=pj: e.matmul(
                            bank[pj][:, :],
                            wkvbf[:, dc * 256 + kvc * 128: dc * 256 + (kvc + 1) * 128],
                            xnch(dc)[:, t0:t0 + NT],
                            start=(dc == 0), stop=(dc == DCH - 1)),
                            reads=[f"wk1{dc}", f"xn{dc}.{tau}"],
                            writes=[f"b{pj}"] if dc == DCH - 1 else [],
                            wait=(pb._dep_waits((), [f"b{pj}"]) if dc == 0 else []) + xnr)
                    rope_evac(bank[pj], f"b{pj}", kT1, kvc * T, f"kT1.{kvc}", t0)
                for mt in range(16):
                    bnk = bank[6 + mt % 2]
                    bkkey = f"b{6 + mt % 2}"
                    for dc in range(DCH):
                        pb.op("tensor", lambda e, dc=dc, mt=mt, kvc=kvc, bnk=bnk: e.matmul(
                            bnk[:, 0:128],
                            xnch(dc)[:, mt * 128:(mt + 1) * 128],
                            wkvbf[:, DCH * 256 + dc * 256 + kvc * 128:
                                  DCH * 256 + dc * 256 + (kvc + 1) * 128],
                            start=(dc == 0), stop=(dc == DCH - 1)),
                            reads=[f"wv1{dc}", f"xn{dc}.{mt // 4}"],
                            writes=[bkkey] if dc == DCH - 1 else [],
                            wait=(pb._dep_waits((), [bkkey]) if dc == 0 else []) + xnr)
                    for gi in range(2):
                        g = 2 * kvc + gi
                        pb.op("scalar", lambda e, mt=mt, g=g, gi=gi, bnk=bnk: e.copy(
                            vG[:, g * 1040 + mt * 65: g * 1040 + mt * 65 + 64],
                            bnk[:, gi * 64:(gi + 1) * 64]),
                            reads=[bkkey, "vGones"], writes=[f"vG{g}"])
            for kvc in range(2):
                pb.op("vector", lambda e, kvc=kvc: e.tensor_tensor(
                    expb[:, 0:T], kT1[:, kvc * T:(kvc + 1) * T],
                    kT1[:, kvc * T:(kvc + 1) * T], ALU.mult),
                    reads=[f"kT1.{kvc}"], writes=["e0", "e1", "e2", "e3"])
                for tau in range(NTAU):
                    pb.op("tensor", lambda e, tau=tau: e.matmul(
                        bank[4][0:1, :], onescol[:, :],
                        expb[:, tau * NT:(tau + 1) * NT], start=True, stop=True),
                        reads=[f"e{tau}", "onescol"], writes=["b4"])
                    pb.op("vector", lambda e, tau=tau: e.reduce_max(
                        CP[0:1, tau:tau + 1], bank[4][0:1, :], AX.X),
                        reads=["b4"], writes=[f"cp{tau}"])
                pb.op("vector", lambda e, kvc=kvc: e.reduce_max(
                    CP[0:1, 16 + kvc:17 + kvc], CP[0:1, 0:4], AX.X),
                    reads=[f"cp{i}" for i in range(4)], writes=[f"knc{kvc}"])

            bctx.close()
            pb.fence()
            cmask = lsb("cmaskS", [128, 4 * NT], BF16)
            wq1bf = lsb("wq1bf", [128, DCH * 128], BF16)
            rrow = lsb("rrow1", [1, 2 * NT], F32)
            rhl1 = lsb("rhl1", [1, 4 * NT], BF16)
            pb.dma("sync", cmask[:, :], cmask_d[:, :], "dmisc", writes=["cmask"])
            PAIR_ORDER = [0, 4, 1, 5, 2, 6, 3, 7, 8, 12, 9, 13, 10, 14, 11, 15]
            wq_r = wq1_d.rearrange("(c p) n -> p c n", p=128)
            for hp in range(DCH):
                pb.dma("sync", stg[:, 0:1024].rearrange("p (c n) -> p c n", n=128),
                       wq_r[:, :, hp * 128:(hp + 1) * 128], "stga",
                       writes=["stg0"])
                pb.op("vector", lambda e: e.tensor_scalar(
                    wq1bf[:, :], stg[:, 0:1024], float(1.0 / 8.0), None, ALU.mult),
                    reads=["stg0"], writes=["wq1bf"])
                for tau in range(NTAU):
                    t0 = tau * NT
                    pj = 6 + tau % 2
                    for dc in range(DCH):
                        pb.op("tensor", lambda e, dc=dc, t0=t0, pj=pj: e.matmul(
                            bank[pj][:, :], wq1bf[:, dc * 128:(dc + 1) * 128],
                            xnch(dc)[:, t0:t0 + NT],
                            start=(dc == 0), stop=(dc == DCH - 1)),
                            reads=["wq1bf", f"xn{dc}.{tau}"],
                            writes=[f"b{pj}"] if dc == DCH - 1 else [],
                            wait=(pb._dep_waits((), [f"b{pj}"]) if dc == 0 else []) + xnr)
                    rope_evac(bank[pj], f"b{pj}", qT1, 0, "qT1", t0)
                pb.op("vector", lambda e: e.tensor_tensor(
                    expb[:, 0:T], qT1[:, :], qT1[:, :], ALU.mult),
                    reads=["qT1"], writes=["e0", "e1", "e2", "e3"])
                for tau in range(NTAU):
                    pb.op("tensor", lambda e, tau=tau: e.matmul(
                        bank[4][0:1, :], onescol[:, :],
                        expb[:, tau * NT:(tau + 1) * NT], start=True, stop=True),
                        reads=[f"e{tau}", "onescol"], writes=["b4"])
                    pb.op("vector", lambda e, tau=tau: e.reduce_max(
                        CP[0:1, tau:tau + 1], bank[4][0:1, :], AX.X),
                        reads=["b4"], writes=[f"cp{tau}"])
                pb.op("vector", lambda e: e.reduce_max(CP[0:1, 8:9], CP[0:1, 0:4], AX.X),
                      reads=[f"cp{i}" for i in range(4)], writes=["cpq"])
                kchunk = hp // 4
                pb.op("vector", lambda e, kchunk=kchunk: e.tensor_tensor(
                    CP[0:1, 10:11], CP[0:1, 8:9], CP[0:1, 16 + kchunk:17 + kchunk],
                    ALU.mult),
                    reads=["cpq", f"knc{kchunk}"], writes=["cpm"])
                pb.op("scalar", lambda e: e.activation(
                    CP[0:1, 10:11], CP[0:1, 10:11], AF.Ln),
                    reads=["cpm"], writes=["cpm"])
                pb.op("scalar", lambda e: e.activation(
                    CP[0:1, 10:11], CP[0:1, 10:11], AF.Exp, scale=0.5),
                    reads=["cpm"], writes=["cpm"])
                pb.op("vector", lambda e: e.tensor_scalar(
                    CP[0:1, 11:12], CP[0:1, 10:11], -1.0, None, ALU.mult),
                    reads=["cpm"], writes=["cpn"])
                pb.op("tensor", lambda e: e.matmul(
                    bank[4][:, 0:1], onesf[:, :], CP[0:1, 11:12],
                    start=True, stop=True),
                    reads=["cpn", "onesf"], writes=["b4"])
                pb.op("vector", lambda e: e.tensor_copy(CP[:, 13:14], bank[4][:, 0:1]),
                      reads=["b4"], writes=["neg1"])

                def mk_fin1(h12, ytgt, ybase, t0):
                    def fin():
                        avb = 4 + h12
                        bb = 6 + h12
                        hi = rhl1[0:1, (2 * h12) * NT:(2 * h12 + 1) * NT]
                        lo = rhl1[0:1, (2 * h12 + 1) * NT:(2 * h12 + 2) * NT]
                        rr = rrow[0:1, h12 * NT:(h12 + 1) * NT]
                        pb.op("scalar", lambda e: e.activation(
                            rr, bank[avb][64:65, :], AF.Ln),
                            reads=[f"b{avb}"], writes=[f"rrow{h12}"])
                        pb.op("scalar", lambda e: e.activation(
                            rr, rr, AF.Exp, scale=-1.0),
                            reads=[f"rrow{h12}"], writes=[f"rrow{h12}"])
                        pb.op("vector", lambda e: e.tensor_copy(hi, rr),
                              reads=[f"rrow{h12}"], writes=[f"rhi{h12}"])
                        pb.op("vector", lambda e: e.tensor_tensor(lo, rr, hi, ALU.subtract),
                              reads=[f"rrow{h12}", f"rhi{h12}"], writes=[f"rlo{h12}"])
                        pb.op("tensor", lambda e: e.matmul(
                            bank[bb][:, :], onesb[0:1, 0:128], hi, start=True, stop=False),
                            reads=[f"rhi{h12}", "onesb"],
                            wait=pb._dep_waits((), [f"b{bb}"]))
                        pb.op("tensor", lambda e: e.matmul(
                            bank[bb][:, :], onesb[0:1, 0:128], lo, start=False, stop=True),
                            reads=[f"rlo{h12}", "onesb"], writes=[f"b{bb}"])
                        rbs = rb2[:, h12 * NT:(h12 + 1) * NT]
                        pb.op("vector", lambda e: e.tensor_copy(rbs, bank[bb][:, :]),
                              reads=[f"b{bb}"], writes=[f"rb{h12}"])
                        pb.op("vector", lambda e, avb=avb, rbs=rbs: e.tensor_tensor(
                            ych(ytgt)[ybase:ybase + 64, t0:t0 + NT],
                            bank[avb][0:64, :], rbs[0:64, :], ALU.mult),
                            reads=[f"b{avb}", f"rb{h12}"], writes=[f"y{ytgt}"])
                    return fin

                hA = PAIR_ORDER[2 * hp]
                hB = PAIR_ORDER[2 * hp + 1]
                gA, gB = hA // 4, hB // 4
                assert (gA % 2, gB % 2) == (0, 1) and gA // 2 == kchunk and gB // 2 == kchunk
                tiles = []
                for tau in range(NTAU):
                    t0 = tau * NT
                    tiles.append(dict(
                        nkc=4 * tau + 4,
                        klhsT=(lambda kc, h12, kchunk=kchunk:
                               kT1[64 * h12:64 * h12 + 64,
                                   kchunk * T + kc * 128: kchunk * T + (kc + 1) * 128]),
                        qrhs=[qT1[0:64, t0:t0 + NT], qT1[64:128, t0:t0 + NT]],
                        kreads=[f"kT1.{kchunk}"], qkey="qT1",
                        vap=(lambda kc, h12, gA=gA, gB=gB:
                             vG[:, (gA if h12 == 0 else gB) * 1040 + kc * 65:
                                (gA if h12 == 0 else gB) * 1040 + (kc + 1) * 65]),
                        vkeys=[f"vG{gA}", f"vG{gB}"], expb=expb, cmask=cmask,
                        mask_from=4 * tau,
                        negcol=CP[:, 13:14], ckey="neg1",
                        fins=[mk_fin1(0, hA // 2, (hA % 2) * 64, t0),
                              mk_fin1(1, hB // 2, (hB % 2) * 64, t0)],
                    ))
                attn_stream(tiles)
            lctx.close()
            pb.fence()
            wctx = ExitStack()
            wsb = lambda name, shape, dt: wctx.enter_context(nc.sbuf_tensor(name, shape, dt))
            wobf = wsb("wobf", [128, DCH * 1024], BF16)
            wtm = wsb("wtmp1", [128, 2 * NT], F32)
            wo_r = wo1_d.rearrange("(c p) n -> c p n", p=128)
            load_cast(wobf, lambda i: wo_r[i], DCH, 1024, "wo1")
            wo_units = []
            for dc in range(DCH):
                for tau in range(NTAU):
                    def wo_unit(dc=dc, tau=tau):
                        t0 = tau * NT
                        bnk = bank[tau % 2]
                        bkkey = f"b{tau % 2}"
                        wslot = tau % 2
                        for yc in range(DCH):
                            pb.op("tensor", lambda e, yc=yc, dc=dc, t0=t0, bnk=bnk: e.matmul(
                                bnk[:, :],
                                wobf[:, yc * 1024 + dc * 128: yc * 1024 + (dc + 1) * 128],
                                ych(yc)[:, t0:t0 + NT],
                                start=(yc == 0), stop=(yc == DCH - 1)),
                                reads=[f"wo1{yc}", f"y{yc}"],
                                writes=[bkkey] if yc == DCH - 1 else [],
                                wait=(pb._dep_waits((), [bkkey]) if yc == 0 else []))
                        pb.op("vector", lambda e, bnk=bnk, wslot=wslot: e.tensor_copy(
                            wtm[:, wslot * NT:(wslot + 1) * NT], bnk[:, :]),
                            reads=[bkkey], writes=[f"wtm{wslot}"])
                        pb.op("vector", lambda e, dc=dc, t0=t0, wslot=wslot: e.tensor_tensor(
                            xch(dc)[:, t0:t0 + NT], xch(dc)[:, t0:t0 + NT],
                            wtm[:, wslot * NT:(wslot + 1) * NT], ALU.add),
                            reads=[f"wtm{wslot}", f"x{dc}"], writes=[f"x{dc}"])
                    wo_units.append(wo_unit)
            return (yctx, wctx), wo_units

        def emit_ffn1(xn_done, wo_units=(), wctx=None):
            lctx = ExitStack()
            lsb = lambda name, shape, dt: lctx.enter_context(nc.sbuf_tensor(name, shape, dt))
            wtf = lsb("wtf", [128, 2 * 1024], F32)
            wabs = lsb("wabs", [128, 512], F32)
            wmask = lsb("wmask", [128, 1024], BF16)
            wsgn = lsb("wsgn", [128, 1024], BF16)
            wtb = lsb("wtb", [128, DCH * 1024], BF16)
            bi8 = lsb("bi8", [NEXP, 128], F32)
            bcolT = lsb("bcolT", [128, 8], F32)
            frow = lsb("frow", [1, 16], F32)
            onescf = lsb("onescf", [128, 1], F32)
            ftm = lsb("ftm", [128, 2 * NT], F32)
            idf32x = lsb("idf32x", [128, 128], F32)
            CP = colpool
            xnr = [xn_done]
            wt_r = wtT_d.rearrange("(c p) n -> c p n", p=128)
            for dc in range(DCH):
                sl = dc % 2
                pb.dma("sync", wtf[:, sl * 1024:(sl + 1) * 1024], wt_r[dc], "dmisc",
                       writes=[f"wtf{sl}"])
                pb.op("vector", lambda e, dc=dc, sl=sl: e.reduce_sum(
                    CP[:, 20 + dc:21 + dc], wtf[:, sl * 1024:(sl + 1) * 1024],
                    AX.X, apply_absolute_value=True),
                    reads=[f"wtf{sl}"], writes=[f"cs{dc}"])
            pb.dma("sync", bi8[:, :], bi1_d.rearrange("x (c n) -> (x c) n", n=128), "dmisc",
                   writes=["bi8"])
            pb.dma("sync", frow[0:1, 0:1], al1_d[0:1, 0:1], "dmisc", writes=["alpha"])
            pb.dma("sync", idf32x[:, :], idf32_d[:, :], "dmisc", writes=["idf32x"])
            pb.op("vector", lambda e: e.memset(onescf[:], 1.0), writes=["onescf"])
            pb.op("tensor", lambda e: e.matmul(
                bank[4][0:1, 0:8], onescf[:, :], CP[:, 20:28],
                start=True, stop=True),
                reads=["onescf"] + [f"cs{i}" for i in range(8)], writes=["b4"])
            pb.op("vector", lambda e: e.tensor_copy(frow[0:1, 8:16], bank[4][0:1, 0:8]),
                  reads=["b4"], writes=["frow8"])
            pb.op("vector", lambda e: e.reduce_sum(
                frow[0:1, 1:2], frow[0:1, 8:16], AX.X),
                reads=["frow8"], writes=["dsum"])
            pb.op("vector", lambda e: e.tensor_scalar(
                frow[0:1, 1:2], frow[0:1, 1:2], float(0.7 / (D * D)), None, ALU.mult),
                reads=["dsum"], writes=["dsum"])
            for (src_sl, cpcol, kk) in ((frow[0:1, 1:2], 28, "deltacol"),
                                        (frow[0:1, 0:1], 29, "alphacol")):
                pb.op("tensor", lambda e, src_sl=src_sl: e.matmul(
                    bank[4][:, 0:1], onesf[:, :], src_sl, start=True, stop=True),
                    reads=["dsum", "alpha", "onesf"], writes=["b4"])
                pb.op("vector", lambda e, cpcol=cpcol: e.tensor_copy(
                    CP[:, cpcol:cpcol + 1], bank[4][:, 0:1]),
                    reads=["b4"], writes=[kk])
            pb.op("tensor", lambda e: e.transpose(
                bank[4][:, 0:8], bi8[0:NEXP, :], idf32x[0:NEXP, 0:NEXP]),
                reads=["bi8", "idf32x"], writes=["b4"])
            pb.op("vector", lambda e: e.tensor_copy(bcolT[:, :], bank[4][:, 0:8]),
                  reads=["b4"], writes=["bcolT"])
            for dc in range(DCH):
                sl = dc % 2
                pb.dma("sync", wtf[:, sl * 1024:(sl + 1) * 1024], wt_r[dc], "dmisc",
                       writes=[f"wtf{sl}"])
                pb.op("scalar", lambda e, dc=dc, sl=sl: e.sign(
                    wsgn[:, :], wtf[:, sl * 1024:(sl + 1) * 1024]),
                    reads=[f"wtf{sl}"], writes=["wsgn"])
                for hh in range(2):
                    pb.op("scalar", lambda e, sl=sl, hh=hh: e.activation(
                        wabs[:, :], wtf[:, sl * 1024 + hh * 512: sl * 1024 + (hh + 1) * 512],
                        AF.Abs),
                        reads=[f"wtf{sl}"], writes=["wabs"])
                    pb.op("vector", lambda e, hh=hh: e.tensor_scalar(
                        wmask[:, hh * 512:(hh + 1) * 512], wabs[:, :],
                        CP[:, 28:29], None, ALU.is_gt),
                        reads=["wabs", "deltacol"], writes=["wmask"])
                pb.op("vector", lambda e, dc=dc: e.tensor_tensor(
                    wtb[:, dc * 1024:(dc + 1) * 1024], wsgn[:, :], wmask[:, :],
                    ALU.mult),
                    reads=["wsgn", "wmask"], writes=[f"wtb{dc}"])
            wo_units = list(wo_units)
            ui = 0
            for oc in range(DCH):
                for tau in range(NTAU):
                    t0 = tau * NT
                    bnk = bank[2 + tau % 2]
                    bkkey = f"b{2 + tau % 2}"
                    for dc in range(DCH):
                        pb.op("tensor", lambda e, dc=dc, oc=oc, t0=t0, bnk=bnk: e.matmul(
                            bnk[:, :],
                            wtb[:, dc * 1024 + oc * 128: dc * 1024 + (oc + 1) * 128],
                            xnch(dc)[:, t0:t0 + NT],
                            start=(dc == 0), stop=(dc == DCH - 1)),
                            reads=[f"wtb{dc}", f"xn{dc}.{tau}"],
                            writes=[bkkey] if dc == DCH - 1 else [],
                            wait=(pb._dep_waits((), [bkkey]) if dc == 0 else []) + xnr)
                    fslot = tau % 2
                    pb.op("vector", lambda e, bnk=bnk, oc=oc, fslot=fslot: e.tensor_scalar(
                        ftm[:, fslot * NT:(fslot + 1) * NT], bnk[:, :],
                        CP[:, 29:30], bcolT[:, oc:oc + 1], ALU.mult, ALU.add),
                        reads=[bkkey, "alphacol", "bcolT"], writes=[f"ftm{fslot}"])
                    pb.op("vector", lambda e, oc=oc, t0=t0, fslot=fslot: e.tensor_tensor(
                        xch(oc)[:, t0:t0 + NT], xch(oc)[:, t0:t0 + NT],
                        ftm[:, fslot * NT:(fslot + 1) * NT], ALU.add),
                        reads=[f"ftm{fslot}", f"x{oc}"], writes=[f"x{oc}"])
                    if ui < len(wo_units):
                        wo_units[ui]()
                        ui += 1
            while ui < len(wo_units):
                wo_units[ui]()
                ui += 1
            lctx.close()
            if wctx is not None:
                wctx[1].close()
                wctx[0].close()
            pb.fence()

        xT_ready = [din_inputs]

        if "ln0" in stages:
            xn_done = emit_layernorm("ln0", xT_ready)
        else:
            xn_done = ("dv", 0)

        if "attn0" in stages:
            emit_attn0(xn_done)

        if "moe0" in stages:
            emit_moe0(xn_done)

        if "ln1" in stages:
            xn1_done = emit_layernorm("ln1", [("dv", pb.cnt["dv"])])
            wo_ctx, wo_units = (None, ())
            if "attn1" in stages:
                wo_ctx, wo_units = emit_attn1(xn1_done)
            if "ffn1" in stages:
                emit_ffn1(xn1_done, wo_units, wo_ctx)
            elif wo_ctx is not None:
                for u in wo_units:
                    u()
                wo_ctx[1].close()
                wo_ctx[0].close()
                pb.fence()

        if dbg == "xn0":
            for dc in range(DCH):
                pb.op("vector", lambda e, dc=dc: e.tensor_copy(xch(dc), xnch(dc)),
                      reads=[f"xn{dc}.{tau}" for tau in range(NTAU)],
                      writes=[f"x{dc}"], wait=[xn_done])

        final_waits = [("dv", pb.cnt["dv"]), ("sc", pb.cnt["sc"]),
                       ("pe", pb.cnt["pe"]), ("gp", pb.cnt["gp"])]
        for dc in range(DCH):
            pb.dma("gpsimd", out_t[dc], xch(dc), "dst",
                   reads=[f"x{dc}"], wait=final_waits)
        pb.on("gpsimd", lambda e: e.engine_nop(), wait=[("dst", pb.cnt["dst"])])

        pb.emit()
    return nc


def _prep_core_inputs(inputs):
    x = np.ascontiguousarray(inputs["L0_attn_x"])
    perm64 = np.concatenate([np.arange(0, 64, 2), np.arange(1, 64, 2)])
    PAIR_ORDER = [0, 4, 1, 5, 2, 6, 3, 7, 8, 12, 9, 13, 10, 14, 11, 15]
    permq = np.concatenate([PAIR_ORDER[i] * 64 + perm64 for i in range(NH)])
    permk = np.concatenate([h * 64 + perm64 for h in range(NKV)])

    wq1 = np.ascontiguousarray(inputs["L1_attn_Wq"][:, permq])
    wk1 = np.ascontiguousarray(inputs["L1_attn_Wk"][:, permk])
    wtT = np.ascontiguousarray(inputs["L1_ffn_weight"].T)

    sel = np.zeros((NEXP, NEXP * 128), np.float32)
    for e in range(NEXP):
        sel[e, e * 128:(e + 1) * 128] = 1.0
    hmask = np.zeros((128, 2), BF)
    hmask[0:64, 0] = 1
    hmask[64:128, 1] = 1
    cmask = np.zeros((128, 4 * NT), BF)
    for o in range(4):
        j = np.arange(128)[:, None]
        i = np.arange(NT)[None, :]
        cmask[:, o * NT:(o + 1) * NT] = (i >= 128 * o + j).astype(BF)

    shared = {
        "l0wq": np.ascontiguousarray(inputs["L0_attn_Wq"]),
        "l0wk": np.ascontiguousarray(inputs["L0_attn_Wk"]),
        "l0bq": np.ascontiguousarray(inputs["L0_attn_bq"]).reshape(1, D),
        "l0bk": np.ascontiguousarray(inputs["L0_attn_bk"]).reshape(1, D),
        "l0wr": np.ascontiguousarray(inputs["L0_ffn_Wr"]),
        "l0w1": np.ascontiguousarray(inputs["L0_ffn_W1"]),
        "l0w2": np.ascontiguousarray(inputs["L0_ffn_W2"]),
        "l1wq": wq1, "l1wk": wk1,
        "l1wv": np.ascontiguousarray(inputs["L1_attn_Wv"]),
        "l1wo": np.ascontiguousarray(inputs["L1_attn_Wo"]),
        "l1wt": wtT,
        "l1alpha": np.ascontiguousarray(inputs["L1_ffn_alpha"]).reshape(1, 1),
        "l1bias": np.ascontiguousarray(inputs["L1_ffn_bias"]).reshape(1, D),
        "sel8": sel, "hmask": hmask, "cmask": cmask,
        "idf32c": np.eye(128, dtype=np.float32),
        "sel2c": np.concatenate([
            np.concatenate([np.ones((1, 128), np.float32), np.zeros((1, 128), np.float32)], 1),
            np.concatenate([np.zeros((1, 128), np.float32), np.ones((1, 128), np.float32)], 1)], 0),
    }
    in_maps = []
    for c in range(NCORES):
        m = dict(shared)
        m["xT"] = np.ascontiguousarray(x[c].T)
        in_maps.append(m)
    return in_maps


def kernel(**inputs):
    in_maps = _prep_core_inputs(inputs)
    nc = build_nc(ALL_STAGES)
    res = run_bass_kernel_spmd(nc, in_maps, core_ids=list(range(NCORES)))
    out = np.empty((NCORES, T, D), np.float32)
    for c in range(NCORES):
        out[c] = res.results[c]["outT"].T
    return out
